# version 02
# baseline (speedup 1.0000x reference)
import math
from contextlib import ExitStack

import numpy as np
import concourse.bass as bass
import concourse.mybir as mybir
from concourse.bass_utils import run_bass_kernel_spmd

F32 = mybir.dt.float32
BF16 = mybir.dt.bfloat16
I32 = mybir.dt.int32
U8 = mybir.dt.uint8
AF = mybir.ActivationFunctionType
ALU = mybir.AluOpType
AX = mybir.AxisListType

NC = 8
L1_STOP = None
L1_SKIP = ()
T = 2048
C = 1024
NT = T // 128
RC = float(np.float32(1.5 * 2 ** 23))
VELT = float(2 ** 22 + 1)


class FW:
    SEC = {"pe": "tensor", "act": "scalar", "dve": "vector", "pool": "gpsimd", "sp": "sync"}
    DQ = ("sp", "pool", "act")

    def __init__(self, nc, ctx):
        self.nc = nc
        self.sems = {}
        for e in list(self.SEC) + [e + "d" for e in self.DQ]:
            self.sems[e] = ctx.enter_context(nc.semaphore("S_" + e))
        self.cnt = {e: 0 for e in self.SEC}
        self.dcnt = {e: 0 for e in self.DQ}
        self.q = {e: [] for e in self.SEC}
        self.hw = {e: {} for e in self.SEC}
        self.res = {}

    def _key(self, r):
        if isinstance(r, (str, tuple)):
            return r
        t = getattr(r, "tensor", None)
        if t is not None:
            return id(t)
        return id(r)

    def _want(self, eng, waits, sem, val):
        if val <= 0:
            return
        if sem == eng and eng == "pe":
            return
        if sem == eng and self.cnt[eng] - val >= 12:
            return
        if self.hw[eng].get(sem, 0) >= val:
            return
        waits[sem] = max(waits.get(sem, 0), val)

    def emit(self, eng, fn, reads=(), writes=(), dma=False, pbase=0, obase=0):
        waits = {}
        if eng == "pe":
            if pbase != getattr(self, "pe_base", 0) and self.cnt["pe"] > 0:
                waits["pe"] = self.cnt["pe"]
            self.pe_base = pbase
        for r in reads:
            e = self.res.get(self._key(r))
            if e and e["w"]:
                self._want(eng, waits, *e["w"])
        for w_ in writes:
            e = self.res.get(self._key(w_))
            if e:
                if e["w"]:
                    self._want(eng, waits, *e["w"])
                for s, v in e["r"].items():
                    self._want(eng, waits, s, v)
        for s, v in waits.items():
            self.hw[eng][s] = max(self.hw[eng].get(s, 0), v)
        if dma:
            self.dcnt[eng] += 1
            sem, val, inc = eng + "d", 16 * self.dcnt[eng], 16
        else:
            self.cnt[eng] += 1
            sem, val, inc = eng, self.cnt[eng], 1
        for w_ in writes:
            self.res[self._key(w_)] = {"w": (sem, val), "r": {}}
        for r in reads:
            e = self.res.setdefault(self._key(r), {"w": None, "r": {}})
            e["r"][sem] = max(e["r"].get(sem, 0), val)
        self.q[eng].append((sorted(waits.items()), fn, (sem, inc)))

    def raw_wait(self, eng, sem, val):
        self.q[eng].append(([(sem, val)], None, None))

    def drain_outputs(self):
        for e in self.DQ:
            if self.dcnt[e]:
                self.raw_wait(e, e + "d", 16 * self.dcnt[e])

    def finalize(self, block):
        for eng, secname in self.SEC.items():
            thunks = self.q[eng]
            if not thunks:
                continue
            def body(e, thunks=thunks):
                for waits, fn, inc in thunks:
                    for s, v in waits:
                        e.wait_ge(self.sems[s], v)
                    if fn is not None:
                        inst = fn(e)
                        inst.then_inc(self.sems[inc[0]], inc[1])
            getattr(block, secname)(body)



def q_e4m3(fw, sc, tmp, n):
    dve = "dve"
    a = lambda i: tmp[:, i * n:(i + 1) * n]
    bits = sc.bitcast(I32)
    e_i = a(0).bitcast(I32)
    fw.emit(dve, lambda e: e.tensor_scalar(e_i, bits, 0x7F800000, None, ALU.bitwise_and),
            reads=[sc], writes=[tmp])
    p2 = a(0)
    rp = a(1)
    fw.emit(dve, lambda e: e.reciprocal(rp, p2), reads=[tmp], writes=[tmp])
    t = a(2)
    fw.emit(dve, lambda e: e.tensor_tensor(t, sc, rp, ALU.mult),
            reads=[sc, tmp], writes=[tmp])
    fw.emit(dve, lambda e: e.tensor_scalar(t, t, 8.0, RC, ALU.mult, ALU.add),
            reads=[tmp], writes=[tmp])
    fw.emit(dve, lambda e: e.tensor_scalar(t, t, -RC, 0.125, ALU.add, ALU.mult),
            reads=[tmp], writes=[tmp])
    nrm = a(1)
    fw.emit(dve, lambda e: e.tensor_tensor(nrm, t, p2, ALU.mult),
            reads=[tmp], writes=[tmp])
    low = a(2)
    fw.emit(dve, lambda e: e.tensor_scalar(low, sc, 512.0, RC, ALU.mult, ALU.add),
            reads=[sc, tmp], writes=[tmp])
    fw.emit(dve, lambda e: e.tensor_scalar(low, low, -RC, 1.0 / 512.0, ALU.add, ALU.mult),
            reads=[tmp], writes=[tmp])
    msk = a(3).bitcast(U8)[:, :n]
    fw.emit(dve, lambda e: e.tensor_scalar(msk, sc, 0.015625, None, ALU.is_lt),
            reads=[sc], writes=[tmp])
    fw.emit(dve, lambda e: e.select(sc, msk, low, nrm),
            reads=[tmp], writes=[sc])


def q_fp4_signed(fw, z, out_bf, dqx, tmp, mtmp, n):
    dve = "dve"
    a = lambda i: tmp[:, i * n:(i + 1) * n]
    h1 = a(0)
    fw.emit(dve, lambda e: e.tensor_scalar(h1, z, 2.0, RC, ALU.mult, ALU.add),
            reads=[z], writes=[tmp])
    fw.emit(dve, lambda e: e.tensor_scalar(h1, h1, -RC, 0.5, ALU.add, ALU.mult),
            reads=[tmp], writes=[tmp])
    cc = a(1)
    fw.emit(dve, lambda e: e.tensor_scalar(cc, z, VELT, None, ALU.mult),
            reads=[z], writes=[tmp])
    dd = a(2)
    fw.emit(dve, lambda e: e.tensor_tensor(dd, cc, z, ALU.subtract),
            reads=[z, tmp], writes=[tmp])
    fw.emit(dve, lambda e: e.tensor_tensor(cc, cc, dd, ALU.subtract),
            reads=[tmp], writes=[tmp])
    az = mtmp[:, :n]
    fw.emit("act", lambda e: e.activation(az, z, AF.Abs), reads=[z], writes=[mtmp])
    m2 = tmp.bitcast(U8)[:, 8 * n:9 * n]
    fw.emit(dve, lambda e: e.tensor_scalar(m2, az, 2.0, None, ALU.is_ge),
            reads=[mtmp], writes=[tmp])
    fw.emit(dve, lambda e: e.select(z, m2, cc, h1),
            reads=[tmp, mtmp], writes=[z])
    fw.emit(dve, lambda e: e.tensor_scalar(z, z, 6.0, -6.0, ALU.min, ALU.max),
            reads=[z], writes=[z])
    fw.emit(dve, lambda e: e.tensor_tensor(out_bf, z, dqx, ALU.mult),
            reads=[z, dqx], writes=[out_bf])


def emit_quant_tile(fw, xin, out_bf, gs6_col, rgs_col, sb, n):
    dve = "dve"
    blocks = n // 16
    absx, bm, sc, dq = sb["absx"], sb["bm"], sb["sc"], sb["dq"]
    dqe, rdqe = sb["dqe"], sb["rdqe"]
    fw.emit("act", lambda e: e.activation(absx[:, :n], xin, AF.Abs),
            reads=[xin], writes=[absx])
    v = absx[:, :n].rearrange("p (b s) -> p b s", s=16)
    h8 = bm[:, :blocks * 8].rearrange("p (b s) -> p b s", s=8)
    fw.emit(dve, lambda e: e.tensor_tensor(h8, v[:, :, 0:8], v[:, :, 8:16], ALU.max),
            reads=[absx], writes=[bm])
    h4 = bm[:, blocks * 8:blocks * 12].rearrange("p (b s) -> p b s", s=4)
    fw.emit(dve, lambda e: e.tensor_tensor(h4, h8[:, :, 0:4], h8[:, :, 4:8], ALU.max),
            reads=[bm], writes=[bm])
    h2 = bm[:, blocks * 12:blocks * 14].rearrange("p (b s) -> p b s", s=2)
    fw.emit(dve, lambda e: e.tensor_tensor(h2, h4[:, :, 0:2], h4[:, :, 2:4], ALU.max),
            reads=[bm], writes=[bm])
    bmx = bm[:, blocks * 14:blocks * 15]
    fw.emit(dve, lambda e: e.tensor_tensor(
        bmx.rearrange("p (b s) -> p b s", s=1), h2[:, :, 0:1], h2[:, :, 1:2], ALU.max),
        reads=[bm], writes=[bm])
    fw.emit(dve, lambda e: e.tensor_scalar(sc[:, :blocks], bmx, gs6_col, 448.0,
                                           ALU.mult, ALU.min),
            reads=[bm, gs6_col], writes=[sc])
    q_e4m3(fw, sc[:, :blocks], sb["q8t"][:, :], blocks)
    fw.emit(dve, lambda e: e.tensor_scalar(dq[:, :blocks], sc[:, :blocks], rgs_col,
                                           None, ALU.mult),
            reads=[sc, rgs_col], writes=[dq])
    eq0 = bm[:, :blocks]
    fw.emit(dve, lambda e: e.tensor_scalar(eq0, dq[:, :blocks], 0.0, None, ALU.is_equal),
            reads=[dq], writes=[bm])
    safe = bm[:, blocks:2 * blocks]
    fw.emit(dve, lambda e: e.tensor_tensor(safe, dq[:, :blocks], eq0, ALU.add),
            reads=[dq, bm], writes=[bm])
    rdq = bm[:, 2 * blocks:3 * blocks]
    fw.emit(dve, lambda e: e.reciprocal(rdq, safe), reads=[bm], writes=[bm])
    fw.emit(dve, lambda e: e.tensor_copy(
        dqe[:, :n].rearrange("p (b s) -> p b s", s=16),
        dq[:, :blocks].unsqueeze(2).broadcast_to([128, blocks, 16])),
        reads=[dq], writes=[dqe])
    fw.emit(dve, lambda e: e.tensor_copy(
        rdqe[:, :n].rearrange("p (b s) -> p b s", s=16),
        rdq.unsqueeze(2).broadcast_to([128, blocks, 16])),
        reads=[bm], writes=[rdqe])
    z = sb["zz"][:, :n]
    fw.emit(dve, lambda e: e.tensor_tensor(z, xin, rdqe[:, :n], ALU.mult),
            reads=[xin, rdqe], writes=[sb["zz"]])
    q_fp4_signed(fw, z, out_bf, dqe[:, :n], sb["q4t"][:, :], sb["absx"][:, n:2 * n], n)



def build_W():
    nc = bass.Bass("TRN2", target_bir_lowering=False)
    ins, outs = {}, {}
    for nm, full, shard in (("g", (4096, 1024), (512, 1024)),
                            ("u", (4096, 1024), (512, 1024)),
                            ("d", (1024, 4096), (128, 4096))):
        ins[nm + "f"] = nc.dram_tensor(nm + "f", list(full), F32, kind="ExternalInput").ap()
        ins[nm + "s"] = nc.dram_tensor(nm + "s", list(shard), F32, kind="ExternalInput").ap()
        outs[nm + "q"] = nc.dram_tensor(nm + "q", list(shard), BF16, kind="ExternalOutput").ap()
    IDF = nc.dram_tensor("IDF", [128, 128], F32, kind="ExternalInput").ap()
    with ExitStack() as ctx:
        E = ctx.enter_context
        fw = FW(nc, ctx)
        wt = E(nc.sbuf_tensor("wt", [128, 4096], F32))
        wt2 = E(nc.sbuf_tensor("wt2", [128, 4096], F32))
        ob = E(nc.sbuf_tensor("ob", [128, 4096], BF16))
        mx = E(nc.sbuf_tensor("mx", [128, 1], F32))
        mxr = E(nc.sbuf_tensor("mxr", [128, 2], F32))
        gcol = E(nc.sbuf_tensor("gcol", [128, 2], F32))
        ones1 = E(nc.sbuf_tensor("ones1", [1, 128], F32))
        idf = E(nc.sbuf_tensor("idf", [128, 128], F32))
        sb = {
            "absx": E(nc.sbuf_tensor("q_absx", [128, 2 * 4096], F32)),
            "bm": E(nc.sbuf_tensor("q_bm", [128, 4096], F32)),
            "sc": E(nc.sbuf_tensor("q_sc", [128, 256], F32)),
            "dq": E(nc.sbuf_tensor("q_dq", [128, 256], F32)),
            "dqe": E(nc.sbuf_tensor("q_dqe", [128, 4096], F32)),
            "rdqe": E(nc.sbuf_tensor("q_rdqe", [128, 4096], F32)),
            "q8t": E(nc.sbuf_tensor("q_q8t", [128, 1024], F32)),
            "q4t": E(nc.sbuf_tensor("q_q4t", [128, 3 * 4096], F32)),
            "zz": E(nc.sbuf_tensor("q_zz", [128, 4096], F32)),
        }
        pmax = E(nc.psum_tensor("pmax", [128, 128], F32))
        block = E(nc.Block())

        fw.emit("sp", lambda e: e.dma_start(out=idf[:], in_=IDF), writes=[idf], dma=True)
        fw.emit("dve", lambda e: e.memset(ones1[:], 1.0), writes=[ones1])

        for nm, full, shard in (("g", (4096, 1024), (512, 1024)),
                                ("u", (4096, 1024), (512, 1024)),
                                ("d", (1024, 4096), (128, 4096))):
            R, Cw = full
            fw.emit("dve", lambda e: e.memset(mx[:], 0.0), writes=[mx])
            for ti in range(R // 128):
                w = wt if ti % 2 == 0 else wt2
                src = ins[nm + "f"][ti * 128:(ti + 1) * 128, :]
                fw.emit("sp", lambda e, w=w, src=src, Cw=Cw: e.dma_start(out=w[:, :Cw], in_=src),
                        writes=[w], dma=True)
                fw.emit("dve", lambda e, w=w, Cw=Cw: e.reduce_max(
                    mxr[:, 0:1], w[:, :Cw], AX.X, apply_absolute_value=True),
                    reads=[w], writes=[mxr])
                fw.emit("dve", lambda e: e.tensor_tensor(mx[:], mx[:], mxr[:, 0:1], ALU.max),
                        reads=[mxr, mx], writes=[mx])
            fw.emit("pe", lambda e: e.transpose(pmax[0:1, :], mx[:], idf[:]),
                    reads=[mx, idf], writes=[pmax])
            fw.emit("dve", lambda e: e.reduce_max(mxr[0:1, 1:2], pmax[0:1, :], AX.X),
                    reads=[pmax], writes=[mxr])
            fw.emit("pe", lambda e: e.matmul(pmax[:, 0:2], ones1[:], mxr[0:1, 0:2],
                                             start=True, stop=True),
                    reads=[mxr, ones1], writes=[pmax])
            fw.emit("dve", lambda e: e.tensor_scalar(gcol[:, 1:2], pmax[:, 1:2],
                                                     1.0 / 2688.0, None, ALU.mult),
                    reads=[pmax], writes=[gcol])
            fw.emit("dve", lambda e: e.reciprocal(mxr[:, 1:2], pmax[:, 1:2]),
                    reads=[pmax], writes=[mxr])
            fw.emit("dve", lambda e: e.tensor_scalar(gcol[:, 0:1], mxr[:, 1:2], 448.0,
                                                     None, ALU.mult),
                    reads=[mxr], writes=[gcol])
            for ti in range(shard[0] // 128):
                w = wt if ti % 2 == 0 else wt2
                src = ins[nm + "s"][ti * 128:(ti + 1) * 128, :]
                fw.emit("sp", lambda e, w=w, src=src, Cw=Cw: e.dma_start(out=w[:, :Cw], in_=src),
                        writes=[w], dma=True)
                emit_quant_tile(fw, w[:, :Cw], ob[:, :Cw], gcol[:, 0:1], gcol[:, 1:2],
                                sb, Cw)
                dst = outs[nm + "q"][ti * 128:(ti + 1) * 128, :]
                fw.emit("sp", lambda e, dst=dst, Cw=Cw: e.dma_start(out=dst, in_=ob[:, :Cw]),
                        reads=[ob], writes=[(nm, "out", ti)], dma=True)
        fw.drain_outputs()
        fw.finalize(block)
    return nc


def run_W(I):
    nc = build_W()
    idf = np.eye(128, dtype=np.float32)
    gw = np.ascontiguousarray(I["L0_ffn_gate_w"], np.float32)
    uw = np.ascontiguousarray(I["L0_ffn_up_w"], np.float32)
    dw = np.ascontiguousarray(I["L0_ffn_down_w"], np.float32)
    in_maps = []
    for c in range(NC):
        in_maps.append({
            "IDF": idf,
            "gf": gw, "uf": uw, "df": dw,
            "gs": np.ascontiguousarray(gw[c * 512:(c + 1) * 512]),
            "us": np.ascontiguousarray(uw[c * 512:(c + 1) * 512]),
            "ds": np.ascontiguousarray(dw[c * 128:(c + 1) * 128]),
        })
    res = run_bass_kernel_spmd(nc, in_maps, core_ids=list(range(NC))).results
    gq = np.concatenate([res[c]["gq"] for c in range(NC)], 0)
    uq = np.concatenate([res[c]["uq"] for c in range(NC)], 0)
    dq = np.concatenate([res[c]["dq"] for c in range(NC)], 0)
    return gq, uq, dq



def ld_xtile(fw, B, xdram, ti):
    par = ti % 2
    slot = B["xstg"][:, par * C:(par + 1) * C]
    key = B["xstg"]
    src = xdram[ti * 128:(ti + 1) * 128, :]
    fw.emit("sp", lambda e, slot=slot, src=src: e.dma_start(out=slot, in_=src),
            reads=[("xdram", ti)], writes=[key], dma=True)
    return slot, key


def st_xtile(fw, B, xdram, ti, src_ap, src_keys):
    dst = xdram[ti * 128:(ti + 1) * 128, :]
    fw.emit("sp", lambda e, dst=dst, src_ap=src_ap: e.dma_start(out=dst, in_=src_ap),
            reads=list(src_keys), writes=[("xdram", ti)], dma=True)


def emit_ln(fw, B, xdram, negmu, rstd):
    st = B["stats"]
    for ti in range(NT):
        xt, xk = ld_xtile(fw, B, xdram, ti)
        fw.emit("dve", lambda e, xt=xt, ti=ti: e.reduce_sum(
            st[:, ti:ti + 1], xt, AX.X), reads=[xk], writes=[st])
        fw.emit("act", lambda e, xt=xt, ti=ti: e.activation(
            B["scrA"][:, :C], xt, AF.Square, accum_out=st[:, 16 + ti:17 + ti]),
            reads=[xk], writes=[B["scrA"], st])
    mu = st[:, 32:48]
    m2 = st[:, 48:64]
    fw.emit("dve", lambda e: e.tensor_scalar(mu, st[:, 0:16], 1.0 / C, None, ALU.mult),
            reads=[st], writes=[st])
    fw.emit("dve", lambda e: e.tensor_scalar(m2, st[:, 16:32], 1.0 / C, None, ALU.mult),
            reads=[st], writes=[st])
    t = st[:, 64:80]
    fw.emit("dve", lambda e: e.tensor_tensor(t, mu, mu, ALU.mult), reads=[st], writes=[st])
    var = st[:, 80:96]
    fw.emit("dve", lambda e: e.tensor_tensor(var, m2, t, ALU.subtract), reads=[st], writes=[st])
    fw.emit("dve", lambda e: e.tensor_scalar(t, var, 1e-5, None, ALU.add),
            reads=[st], writes=[st])
    fw.emit("dve", lambda e: e.reciprocal(st[:, 0:16], t), reads=[st], writes=[st])
    fw.emit("act", lambda e: e.activation(rstd, st[:, 0:16], AF.Sqrt), reads=[st], writes=[rstd])
    fw.emit("dve", lambda e: e.tensor_scalar(negmu, mu, -1.0, None, ALU.mult),
            reads=[st], writes=[negmu])


def emit_fm_build(fw, B, xdram, fm_dst, negmu, rstd):
    psT = B["psT"][:, 1024:1152]
    for ti in range(NT):
        xt, xk = ld_xtile(fw, B, xdram, ti)
        xb = B["scrA"][:, :C]
        fw.emit("dve", lambda e, xt=xt, xb=xb, ti=ti: e.tensor_scalar(
            xb, xt, negmu[:, ti:ti + 1], rstd[:, ti:ti + 1], ALU.add, ALU.mult),
            reads=[xk, negmu, rstd], writes=[B["scrA"]])
        for k in range(8):
            fw.emit("pe", lambda e, xb=xb, k=k: e.transpose(
                psT, xb[:, k * 128:(k + 1) * 128], B["idf"][:, :]),
                reads=[B["scrA"], B["idf"]], writes=[B["PS1"]])
            dst = fm_dst[:, k * T + ti * 128: k * T + (ti + 1) * 128]
            fw.emit("dve", lambda e, dst=dst: e.tensor_copy(dst, psT),
                    reads=[B["PS1"]], writes=[fm_dst])


def bcast_dma(fw, B, dram_row, dst_ap, dst_key, w, cast=False):
    q = "pool" if cast else "sp"
    fw.emit(q, lambda e: e.dma_start(out=dst_ap, in_=dram_row.broadcast_to([128, w])),
            writes=[dst_key], dma=True)



def emit_l0(fw, B, ins, drams):
    st = B["stats"]
    negmu = st[:, 96:112]
    rstd = st[:, 112:128]
    xdram = drams["x"]
    emit_ln(fw, B, xdram, negmu, rstd)

    cols = B["cols"]
    fw.emit("sp", lambda e: e.dma_start(out=B["s2"][0:1, 0:1], in_=ins["s_in"]),
            writes=[B["s2"]], dma=True)
    fw.emit("sp", lambda e: e.dma_start(out=B["s2"][0:1, 1:2], in_=ins["s_in_down"]),
            writes=[B["s2"]], dma=True)
    psc = B["psT"][:, 1024:1026]
    fw.emit("pe", lambda e, psc=psc: e.matmul(psc, B["ones1"][:, :], B["s2"][0:1, 0:2],
                                              start=True, stop=True),
            reads=[B["s2"], B["ones1"]], writes=[B["PS1"]])
    fw.emit("dve", lambda e, psc=psc: e.tensor_scalar(cols[:, 0:1], psc[:, 0:1], 1.0 / 6.0,
                                                      None, ALU.mult),
            reads=[B["PS1"]], writes=[cols])
    fw.emit("dve", lambda e, psc=psc: e.tensor_scalar(cols[:, 2:3], psc[:, 1:2], 1.0 / 6.0,
                                                      None, ALU.mult),
            reads=[B["PS1"]], writes=[cols])
    fw.emit("dve", lambda e, psc=psc: e.reciprocal(cols[:, 1:2], psc[:, 0:1]),
            reads=[B["PS1"]], writes=[cols])
    fw.emit("dve", lambda e, psc=psc: e.reciprocal(cols[:, 3:4], psc[:, 1:2]),
            reads=[B["PS1"]], writes=[cols])

    sb = B["qsb"]
    fm = B["fm"]
    psT = B["psT"][:, 1024:1152]
    for ti in range(NT):
        xt, xk = ld_xtile(fw, B, xdram, ti)
        xn = B["scrA"][:, :C]
        xq = B["scrA"][:, C:C + 512]
        fw.emit("dve", lambda e, xt=xt, xn=xn, ti=ti: e.tensor_scalar(
            xn, xt, negmu[:, ti:ti + 1], rstd[:, ti:ti + 1], ALU.add, ALU.mult),
            reads=[xk, st], writes=[B["scrA"]])
        for hf in range(2):
            emit_quant_tile(fw, xn[:, hf * 512:(hf + 1) * 512], xq,
                            cols[:, 0:1], cols[:, 1:2], sb, 512)
            for k in range(4):
                fw.emit("pe", lambda e, xq=xq, k=k: e.transpose(
                    psT, xq[:, k * 128:(k + 1) * 128], B["idf"][:, :]),
                    reads=[B["scrA"], B["idf"]], writes=[B["PS1"]])
                kk = hf * 4 + k
                dst = fm[:, kk * T + ti * 128: kk * T + (ti + 1) * 128]
                fw.emit("dve", lambda e, dst=dst: e.tensor_copy(dst, psT),
                        reads=[B["PS1"]], writes=[fm])

    wg = B["wgu"]
    wu = B["wgu2"]
    wd = B["wd"]
    psG = B["psG"][:, 0:512]
    psU = B["psU"][:, 512:1024]
    psD = B["psD"]
    for tg in range(8):
        tis = [2 * tg, 2 * tg + 1]
        for hc in range(8):
            src = ins["wqg"].rearrange("(k p) h -> p k h", p=128)[:, :, hc * 512:(hc + 1) * 512]
            fw.emit("sp", lambda e, src=src: e.dma_start(
                out=wg[:, :].rearrange("p (k h) -> p k h", k=8), in_=src),
                writes=[wg], dma=True)
            src = ins["wqu"].rearrange("(k p) h -> p k h", p=128)[:, :, hc * 512:(hc + 1) * 512]
            fw.emit("sp", lambda e, src=src: e.dma_start(
                out=wu[:, :].rearrange("p (k h) -> p k h", k=8), in_=src),
                writes=[wu], dma=True)
            for gi, ti in enumerate(tis):
                for k in range(8):
                    lhs = fm[:, k * T + ti * 128: k * T + (ti + 1) * 128]
                    fw.emit("pe", lambda e, lhs=lhs, k=k: e.matmul(
                        psG, lhs, wg[:, k * 512:(k + 1) * 512],
                        start=(k == 0), stop=(k == 7)),
                        reads=[fm, wg], writes=[B["PS1"]])
                    fw.emit("pe", lambda e, lhs=lhs, k=k: e.matmul(
                        psU, lhs, wu[:, k * 512:(k + 1) * 512],
                        start=(k == 0), stop=(k == 7)),
                        reads=[fm, wu], writes=[B["PS1"]])
                sil = B["scrA"][:, :512]
                fw.emit("act", lambda e, sil=sil: e.activation(sil, psG, AF.Silu),
                        reads=[B["PS1"]], writes=[B["scrA"]])
                hid = B["scrA"][:, 512:1024]
                fw.emit("dve", lambda e, sil=sil, hid=hid: e.tensor_tensor(
                    hid, sil, psU, ALU.mult),
                    reads=[B["scrA"], B["PS1"]], writes=[B["scrA"]])
                hq = B["scrA"][:, 1024:1536]
                emit_quant_tile(fw, hid, hq, cols[:, 2:3], cols[:, 3:4], sb, 512)
                for sj in range(4):
                    fw.emit("pe", lambda e, hq=hq, sj=sj: e.transpose(
                        psT, hq[:, sj * 128:(sj + 1) * 128], B["idf"][:, :]),
                        reads=[B["scrA"], B["idf"]], writes=[B["PS1"]])
                    dstf = B["fmt"][:, gi * 512 + sj * 128: gi * 512 + (sj + 1) * 128]
                    fw.emit("dve", lambda e, dstf=dstf: e.tensor_copy(dstf, psT),
                            reads=[B["PS1"]], writes=[B["fmt"]])
            for sp2 in range(2):
                src = ins["wqd"][hc * 512 + sp2 * 256:hc * 512 + (sp2 + 1) * 256, :]
                src = src.rearrange("(s p) o -> p s o", p=128)
                fw.emit("sp", lambda e, src=src: e.dma_start(
                    out=wd[:, :].rearrange("p (s o) -> p s o", s=2), in_=src),
                    writes=[wd], dma=True)
                for gi2 in range(2):
                    for sl in range(2):
                        sj = sp2 * 2 + sl
                        lhs = B["fmt"][:, gi2 * 512 + sj * 128: gi2 * 512 + (sj + 1) * 128]
                        for nh in range(2):
                            fw.emit("pe", lambda e, lhs=lhs, sl=sl, nh=nh, gi2=gi2,
                                    hc=hc, sp2=sp2: e.matmul(
                                psD[:, gi2 * 1024 + nh * 512: gi2 * 1024 + (nh + 1) * 512],
                                lhs, wd[:, sl * 1024 + nh * 512: sl * 1024 + (nh + 1) * 512],
                                start=(hc == 0 and sp2 == 0 and sl == 0),
                                stop=(hc == 7 and sp2 == 1 and sl == 1)),
                                reads=[B["fmt"], wd], writes=[B["PS2"]])
        for gi, ti in enumerate(tis):
            gt = B["scrB16"][:, :C]
            fw.emit("act", lambda e, gt=gt, gi=gi: e.activation(
                gt, psD[:, gi * 1024:(gi + 1) * 1024], AF.Silu),
                reads=[B["PS2"]], writes=[B["scrB16"]])
            dst = drams["gscr"][ti * 128:(ti + 1) * 128, :]
            fw.emit("sp", lambda e, dst=dst, gt=gt: e.dma_start(out=dst, in_=gt),
                    reads=[B["scrB16"]], writes=[("gscr", ti)], dma=True)

    emit_fm_build(fw, B, xdram, fm, negmu, rstd)

    OT = B["ot"]
    for hp in range(8):
        for wnm, dst in (("WQT", B["wq_p"]), ("WKT", B["wk_p"]), ("WVT", B["wv_p"])):
            src = ins[wnm][:, hp * 128:(hp + 1) * 128].rearrange("(k p) m -> p k m", p=128)
            fw.emit("pool", lambda e, src=src, dst=dst: e.dma_start(
                out=dst[:, :].rearrange("p (k m) -> p k m", k=8), in_=src),
                writes=[dst], dma=True)
        for wsb, qdst in ((B["wq_p"], B["qt2"]), (B["wk_p"], B["kt2"])):
            for nb in range(4):
                for k in range(8):
                    fw.emit("pe", lambda e, wsb=wsb, nb=nb, k=k: e.matmul(
                        B["psA"][:, nb * 512:(nb + 1) * 512],
                        wsb[:, k * 128:(k + 1) * 128],
                        fm[:, k * T + nb * 512: k * T + (nb + 1) * 512],
                        start=(k == 0), stop=(k == 7)),
                        reads=[wsb, fm], writes=[B["PS1"]])
            for nb in range(4):
                fw.emit("act", lambda e, qdst=qdst, nb=nb: e.activation(
                    qdst[:, nb * 512:(nb + 1) * 512], B["psA"][:, nb * 512:(nb + 1) * 512],
                    AF.Copy),
                    reads=[B["PS1"]], writes=[qdst])
        fw.emit("dve", lambda e: e.memset(B["v2a"][:, :], 1.0), writes=[B["v2a"]])
        for ti in range(NT):
            for k in range(8):
                lhs = fm[:, k * T + ti * 128: k * T + (ti + 1) * 128]
                fw.emit("pe", lambda e, lhs=lhs, k=k: e.matmul(
                    B["psV"][:, 0:128], lhs, B["wv_p"][:, k * 128:(k + 1) * 128],
                    start=(k == 0), stop=(k == 7)),
                    reads=[fm, B["wv_p"]], writes=[B["PS1"]])
            for eh in range(2):
                dst = B["v2a"][:, ti * 130 + eh * 65: ti * 130 + eh * 65 + 64]
                fw.emit("dve", lambda e, dst=dst, eh=eh: e.tensor_copy(
                    dst, B["psV"][:, eh * 64:(eh + 1) * 64]),
                    reads=[B["PS1"]], writes=[B["v2a"]])
        for eh in range(2):
            po = 64 * eh
            for jc in range(NT):
                for nb in range(4):
                    fw.emit("pe", lambda e, jc=jc, nb=nb, po=po: e.matmul(
                        B["psA"][:, nb * 512:(nb + 1) * 512],
                        B["kt2"][po:po + 64, jc * 128:(jc + 1) * 128],
                        B["qt2"][po:po + 64, nb * 512:(nb + 1) * 512],
                        start=True, stop=True),
                        reads=[B["kt2"], B["qt2"]], writes=[B["PS1"]])
                fw.emit("act", lambda e: e.activation(B["ptt"][:, :], B["psA"][:, :],
                                                      AF.Exp, scale=0.125),
                        reads=[B["PS1"]], writes=[B["ptt"]])
                for ib in range(4):
                    fw.emit("pe", lambda e, jc=jc, ib=ib, eh=eh: e.matmul(
                        B["psO"][0:65, ib * 512:(ib + 1) * 512],
                        B["v2a"][:, jc * 130 + eh * 65: jc * 130 + (eh + 1) * 65],
                        B["ptt"][:, ib * 512:(ib + 1) * 512],
                        start=(jc == 0), stop=(jc == NT - 1)),
                        reads=[B["v2a"], B["ptt"]], writes=[B["PS2"]])
            fw.emit("dve", lambda e: e.tensor_copy(B["lrow"][0:1, :], B["psO"][64:65, :]),
                    reads=[B["PS2"]], writes=[B["ptt"]])
            fw.emit("act", lambda e: e.activation(B["lrow"][0:1, :], B["lrow"][0:1, :], AF.Ln),
                    reads=[B["ptt"]], writes=[B["ptt"]])
            fw.emit("act", lambda e: e.activation(B["lrow"][0:1, :], B["lrow"][0:1, :],
                                                  AF.Exp, scale=-1.0),
                    reads=[B["ptt"]], writes=[B["ptt"]])
            for ib in range(4):
                fw.emit("pe", lambda e, ib=ib: e.matmul(
                    B["psA"][:, ib * 512:(ib + 1) * 512], B["ones1b"][:, :],
                    B["lrow"][0:1, ib * 512:(ib + 1) * 512], start=True, stop=True),
                    reads=[B["ptt"], B["ones1b"]], writes=[B["PS1"]])
            fw.emit("dve", lambda e: e.tensor_copy(B["rb16"][:, :], B["psA"][:, :]),
                    reads=[B["PS1"]], writes=[B["scrB16"]])
            for ib in range(4):
                dst = OT[64 * eh:64 * eh + 64, hp * T + ib * 512: hp * T + (ib + 1) * 512]
                fw.emit("dve", lambda e, dst=dst, ib=ib: e.tensor_tensor(
                    dst, B["psO"][0:64, ib * 512:(ib + 1) * 512],
                    B["rb16"][0:64, ib * 512:(ib + 1) * 512], ALU.mult),
                    reads=[B["PS2"], B["scrB16"]], writes=[OT])

    bcast_dma(fw, B, ins["BO"][0:1, :], B["bo_b"][:, :], B["bo_b"], C, cast=True)
    for ti in range(NT):
        src = drams["gscr"][ti * 128:(ti + 1) * 128, :]
        gt = B["scrB16"][:, :C]
        fw.emit("sp", lambda e, src=src, gt=gt: e.dma_start(out=gt, in_=src),
                reads=[("gscr", ti)], writes=[B["scrB16"]], dma=True)
        for cc2 in range(8):
            src = ins["WOT"][cc2 * 128:(cc2 + 1) * 128, :]
            fw.emit("pool", lambda e, src=src, cc2=cc2: e.dma_start(
                out=B["wo_c"][:, :], in_=src),
                writes=[B["wo_c"]], dma=True)
            lhs = OT[:, cc2 * T + ti * 128: cc2 * T + (ti + 1) * 128]
            for nh in range(2):
                fw.emit("pe", lambda e, lhs=lhs, nh=nh, cc2=cc2: e.matmul(
                    B["psD"][:, nh * 512:(nh + 1) * 512], lhs,
                    B["wo_c"][:, nh * 512:(nh + 1) * 512],
                    start=(cc2 == 0), stop=(cc2 == 7)),
                    reads=[OT, B["wo_c"]], writes=[B["PS2"]])
        s1 = B["scrA"][:, :C]
        fw.emit("dve", lambda e, s1=s1: e.tensor_tensor(s1, B["psD"][:, 0:1024],
                                                        B["bo_b"][:, :], ALU.add),
                reads=[B["PS2"], B["bo_b"]], writes=[B["scrA"]])
        if "sdbg" in drams:
            dstS = drams["sdbg"][ti * 128:(ti + 1) * 128, :]
            fw.emit("sp", lambda e, dstS=dstS, s1=s1: e.dma_start(out=dstS, in_=s1),
                    reads=[B["scrA"]], writes=[("sdbg", ti)], dma=True)
        fw.emit("dve", lambda e, s1=s1, gt=gt: e.tensor_tensor(s1, s1, gt, ALU.mult),
                reads=[B["scrA"], B["scrB16"]], writes=[B["scrA"]])
        xt, xk = ld_xtile(fw, B, xdram, ti)
        fw.emit("dve", lambda e, xt=xt, s1=s1: e.tensor_tensor(xt, xt, s1, ALU.add),
                reads=[B["scrA"], xk], writes=[xk])
        st_xtile(fw, B, xdram, ti, xt, [xk])



def emit_l1(fw, B, ins, drams):
    st = B["stats"]
    negmu = st[:, 96:112]
    rstd = st[:, 112:128]
    fm = B["fm"]
    xdram = drams["x"]
    psT = B["psT"][:, 1024:1152]
    emit_ln(fw, B, xdram, negmu, rstd)
    emit_fm_build(fw, B, xdram, fm, negmu, rstd)
    if L1_STOP == "fm":
        return

    ccols = {}
    for nm in ("MAAX", "MAAW", "MAAK", "MAAV", "MAAR", "MAAG", "TDEC", "UCOL",
               "FF1G", "FF1B"):
        dst = B["ccols"][:, 8 * len(ccols):8 * len(ccols) + 8]
        fw.emit("sp", lambda e, dst=dst, nm=nm: e.dma_start(out=dst, in_=ins[nm]),
                writes=[(nm,)], dma=True)
        ccols[nm] = (dst, (nm,))
    if "sdbg" in drams:
        fw.emit("sp", lambda e: e.dma_start(out=drams["sdbg"][0:128, 0:96],
                in_=B["ccols"][:, 0:96]),
                reads=[(n_,) for n_ in ccols], dma=True)

    v2c = st[:, 64:80]
    fw.emit("dve", lambda e: e.tensor_tensor(v2c, rstd, rstd, ALU.mult),
            reads=[st], writes=[st])
    fw.emit("dve", lambda e: e.tensor_tensor(v2c, v2c, st[:, 80:96], ALU.mult),
            reads=[st], writes=[st])
    fw.emit("dve", lambda e: e.tensor_scalar(v2c, v2c, 1e-5, None, ALU.add),
            reads=[st], writes=[st])
    fw.emit("dve", lambda e: e.reciprocal(st[:, 0:16], v2c), reads=[st], writes=[st])
    rstd2 = st[:, 16:32]
    fw.emit("act", lambda e: e.activation(rstd2, st[:, 0:16], AF.Sqrt),
            reads=[st], writes=[st])
    r2bc = B["v2a"][:, 0:2048]
    for ti in range(NT):
        fw.emit("pe", lambda e, ti=ti: e.transpose(
            psT[0:1, 0:128], rstd2[:, ti:ti + 1], B["idf"][:, :]),
            reads=[st, B["idf"]], writes=[B["PS1"]])
        fw.emit("dve", lambda e, ti=ti: e.tensor_copy(
            B["lrow"][0:1, ti * 128:(ti + 1) * 128], psT[0:1, 0:128]),
            reads=[B["PS1"]], writes=[B["ptt"]])
    for ti in range(NT):
        fw.emit("pe", lambda e, ti=ti: e.matmul(
            B["psV"][:, 0:128], B["ones1b"][:, :],
            B["lrow"][0:1, ti * 128:(ti + 1) * 128], start=True, stop=True),
            reads=[B["ptt"], B["ones1b"]], writes=[B["PS1"]])
        fw.emit("dve", lambda e, ti=ti: e.tensor_copy(
            r2bc[:, ti * 128:(ti + 1) * 128], B["psV"][:, 0:128]),
            reads=[B["PS1"]], writes=[B["v2a"]])
    gamC, gk = ccols["FF1G"]
    betC, bk = ccols["FF1B"]
    b2h = B["bo_b"]
    bcast_dma(fw, B, ins["FC2B"][0:1, :], b2h[:, :], B["bo_b"], C, cast=True)
    fw.emit("dve", lambda e: e.tensor_scalar(b2h[:, :], b2h[:, :], 0.5, None, ALU.mult),
            reads=[B["bo_b"]], writes=[B["bo_b"]])

    psGa = B["psG"][:, 0:512]
    psD = B["psD"]
    for tg in range(8):
        tis = [2 * tg, 2 * tg + 1]
        for hc in range(8):
            for k in range(8):
                qf = B["scrA"][:, 0:512]
                src = ins["FC1Q"][k * 128:(k + 1) * 128, hc * 512:(hc + 1) * 512]
                fw.emit("pool", lambda e, qf=qf, src=src: e.dma_start(out=qf, in_=src),
                        writes=[B["scrA"]], dma=True)
                sbc = B["scrA"][:, 512:1024]
                src2 = ins["FC1S"][k:k + 1, hc * 512:(hc + 1) * 512].broadcast_to([128, 512])
                fw.emit("sp", lambda e, sbc=sbc, src2=src2: e.dma_start(out=sbc, in_=src2),
                        writes=[B["scrA"]], dma=True)
                dstw = B["wgu"][:, k * 512:(k + 1) * 512]
                fw.emit("dve", lambda e, qf=qf, sbc=sbc, dstw=dstw: e.scalar_tensor_tensor(
                    dstw, qf, -128.0, sbc, ALU.add, ALU.mult),
                    reads=[B["scrA"], B["scrA"]], writes=[B["wgu"]])
            b1b = B["scrA"][:, 1024:1536]
            srcb = ins["FC1B"][0:1, hc * 512:(hc + 1) * 512].broadcast_to([128, 512])
            fw.emit("sp", lambda e, b1b=b1b, srcb=srcb: e.dma_start(out=b1b, in_=srcb),
                    writes=[B["scrA"]], dma=True)
            for gi, ti in enumerate(tis):
                for k in range(8):
                    hslc = B["scrB16"][:, k * 128:(k + 1) * 128]
                    fw.emit("dve", lambda e, hslc=hslc, k=k, ti=ti: e.tensor_tensor(
                        hslc, fm[:, k * T + ti * 128:k * T + (ti + 1) * 128],
                        r2bc[:, ti * 128:(ti + 1) * 128], ALU.mult),
                        reads=[fm, B["v2a"]], writes=[B["scrB16"]])
                    fw.emit("dve", lambda e, hslc=hslc, k=k: e.tensor_scalar(
                        hslc, hslc, gamC[:, k:k + 1], betC[:, k:k + 1], ALU.mult, ALU.add),
                        reads=[B["scrB16"], gk, bk], writes=[B["scrB16"]])
                for k in range(8):
                    fw.emit("pe", lambda e, k=k: e.matmul(
                        psGa, B["scrB16"][:, k * 128:(k + 1) * 128],
                        B["wgu"][:, k * 512:(k + 1) * 512],
                        start=(k == 0), stop=(k == 7)),
                        reads=[B["scrB16"], B["wgu"]], writes=[B["PS1"]])
                hq = B["scrA"][:, 512:1024]
                fw.emit("dve", lambda e, hq=hq, b1b=b1b: e.tensor_tensor(
                    hq, psGa, b1b, ALU.add),
                    reads=[B["PS1"], B["scrA"]], writes=[B["scrA"]])
                fw.emit("act", lambda e, hq=hq: e.activation(hq, hq, AF.Silu),
                        reads=[B["scrA"]], writes=[B["scrA"]])
                for sj in range(4):
                    fw.emit("pe", lambda e, hq=hq, sj=sj: e.transpose(
                        psT, hq[:, sj * 128:(sj + 1) * 128], B["idf"][:, :]),
                        reads=[B["scrA"], B["idf"]], writes=[B["PS1"]])
                    dstf = B["fmt"][:, gi * 512 + sj * 128: gi * 512 + (sj + 1) * 128]
                    fw.emit("dve", lambda e, dstf=dstf: e.tensor_copy(dstf, psT),
                            reads=[B["PS1"]], writes=[B["fmt"]])
            for sp2 in range(2):
                for sl in range(2):
                    row = hc * 4 + sp2 * 2 + sl
                    qf = B["scrA"][:, 0:1024]
                    src = ins["FC2Q"][row * 128:(row + 1) * 128, :]
                    fw.emit("pool", lambda e, qf=qf, src=src: e.dma_start(out=qf, in_=src),
                            writes=[B["scrA"]], dma=True)
                    sbc = B["scrA"][:, 1024:1536]
                    for hh in range(2):
                        src2 = ins["FC2S"][row:row + 1, hh * 512:(hh + 1) * 512]
                        src2 = src2.broadcast_to([128, 512])
                        fw.emit("sp", lambda e, sbc=sbc, src2=src2: e.dma_start(
                            out=sbc, in_=src2), writes=[B["scrA"]], dma=True)
                        dstw = B["wd"][:, sl * 1024 + hh * 512: sl * 1024 + (hh + 1) * 512]
                        fw.emit("dve", lambda e, qf=qf, sbc=sbc, dstw=dstw, hh=hh:
                                e.scalar_tensor_tensor(
                            dstw, qf[:, hh * 512:(hh + 1) * 512], -128.0, sbc,
                            ALU.add, ALU.mult),
                            reads=[B["scrA"], B["scrA"]], writes=[B["wd"]])
                for gi2 in range(2):
                    for sl in range(2):
                        sj = sp2 * 2 + sl
                        lhs = B["fmt"][:, gi2 * 512 + sj * 128: gi2 * 512 + (sj + 1) * 128]
                        for nh in range(2):
                            fw.emit("pe", lambda e, lhs=lhs, sl=sl, nh=nh, gi2=gi2,
                                    hc=hc, sp2=sp2: e.matmul(
                                psD[:, gi2 * 1024 + nh * 512: gi2 * 1024 + (nh + 1) * 512],
                                lhs, B["wd"][:, sl * 1024 + nh * 512: sl * 1024 + (nh + 1) * 512],
                                start=(hc == 0 and sp2 == 0 and sl == 0),
                                stop=(hc == 7 and sp2 == 1 and sl == 1)),
                                reads=[B["fmt"], B["wd"]], writes=[B["PS2"]])
        for gi, ti in enumerate(tis):
            xt, xk = ld_xtile(fw, B, xdram, ti)
            gt = B["scrB16"][:, 0:C]
            for hf in range(2):
                xnh = B["scrA"][:, 0:512]
                fw.emit("dve", lambda e, xnh=xnh, xt=xt, ti=ti, hf=hf: e.tensor_scalar(
                    xnh, xt[:, hf * 512:(hf + 1) * 512], negmu[:, ti:ti + 1],
                    rstd[:, ti:ti + 1], ALU.add, ALU.mult),
                    reads=[xk, st], writes=[B["scrA"]])
                t1 = B["scrA"][:, 512:1024]
                fw.emit("dve", lambda e, t1=t1, gi=gi, hf=hf: e.scalar_tensor_tensor(
                    t1, psD[:, gi * 1024 + hf * 512: gi * 1024 + (hf + 1) * 512], 0.5,
                    b2h[:, hf * 512:(hf + 1) * 512], ALU.mult, ALU.add),
                    reads=[B["PS2"], B["bo_b"]], writes=[B["scrA"]])
                fw.emit("dve", lambda e, t1=t1, xnh=xnh: e.tensor_tensor(
                    t1, t1, xnh, ALU.add),
                    reads=[B["scrA"]], writes=[B["scrA"]])
                fw.emit("act", lambda e, gt=gt, t1=t1, hf=hf: e.activation(
                    gt[:, hf * 512:(hf + 1) * 512], t1, AF.Silu),
                    reads=[B["scrA"]], writes=[B["scrB16"]])
            dst = drams["gscr"][ti * 128:(ti + 1) * 128, :]
            fw.emit("sp", lambda e, dst=dst, gt=gt: e.dma_start(out=dst, in_=gt),
                    reads=[B["scrB16"]], writes=[("gscr", ti)], dma=True)

    if L1_STOP == "ffn":
        return
    def emit_xxs(k, tb, dst, bnd=False):
        if tb == 0:
            fw.emit("dve", lambda e, k=k, dst=dst: e.tensor_tensor(
                dst[:, 1:512], fm[:, k * T: k * T + 511],
                fm[:, k * T + 1: k * T + 512], ALU.subtract),
                reads=[fm], writes=[B["scrB16"]])
            fw.emit("dve", lambda e, k=k, dst=dst: e.tensor_scalar(
                dst[:, 0:1], fm[:, k * T:k * T + 1], -1.0, None, ALU.mult),
                reads=[fm], writes=[B["scrB16"]])
        elif bnd:
            fw.emit("dve", lambda e, k=k, tb=tb, dst=dst: e.tensor_tensor(
                dst[:, 1:512], fm[:, k * T + tb * 512: k * T + tb * 512 + 511],
                fm[:, k * T + tb * 512 + 1: k * T + (tb + 1) * 512], ALU.subtract),
                reads=[fm], writes=[B["scrB16"]])
            fw.emit("dve", lambda e, k=k, tb=tb, dst=dst: e.tensor_tensor(
                dst[:, 0:1], st[:, k * 3 + tb - 1: k * 3 + tb],
                fm[:, k * T + tb * 512: k * T + tb * 512 + 1], ALU.subtract),
                reads=[fm, st], writes=[B["scrB16"]])
        else:
            fw.emit("dve", lambda e, k=k, tb=tb, dst=dst: e.tensor_tensor(
                dst, fm[:, k * T + tb * 512 - 1: k * T + (tb + 1) * 512 - 1],
                fm[:, k * T + tb * 512: k * T + (tb + 1) * 512], ALU.subtract),
                reads=[fm], writes=[B["scrB16"]])
    for k in range(8):
        src = ins["TW1"][k * 128:(k + 1) * 128, :]
        fw.emit("pool", lambda e, k=k, src=src: e.dma_start(
            out=B["w1sb"][:, k * 160:(k + 1) * 160], in_=src),
            writes=[B["w1sb"]], dma=True)
        src = ins["DW1"][k * 128:(k + 1) * 128, :]
        fw.emit("pool", lambda e, k=k, src=src: e.dma_start(
            out=B["dw1sb"][:, k * 64:(k + 1) * 64], in_=src),
            writes=[B["dw1sb"]], dma=True)
    fw.emit("pool", lambda e: e.dma_start(out=B["dw2sb"][0:64, :], in_=ins["DW2"]),
            writes=[B["dw2sb"]], dma=True)

    mixTa, mixTb = B["mixTa"], B["mixTb"]
    xxs = B["scrB16"][:, 0:512]
    for tb in range(4):
        for k in range(8):
            emit_xxs(k, tb, xxs)
            dstm = B["mixbuf"][:, k * 512:(k + 1) * 512]
            fw.emit("dve", lambda e, dstm=dstm, k=k, tb=tb: e.scalar_tensor_tensor(
                dstm, xxs, ccols["MAAX"][0][:, k:k + 1],
                fm[:, k * T + tb * 512: k * T + (tb + 1) * 512], ALU.mult, ALU.add),
                reads=[B["scrB16"], fm, ccols["MAAX"][1]], writes=[B["mixbuf"]])
        for mi, (mo, mw_) in enumerate(((0, 96), (96, 160))):
            for k in range(8):
                fw.emit("pe", lambda e, mo=mo, mw_=mw_, k=k: e.matmul(
                    B["PS1"][0:mw_ - mo, 0:512],
                    B["w1sb"][:, k * 160 + mo: k * 160 + mw_],
                    B["mixbuf"][:, k * 512:(k + 1) * 512],
                    start=(k == 0), stop=(k == 7)),
                    reads=[B["w1sb"], B["mixbuf"]], writes=[B["PS1"]])
            dst = mixTa if mi == 0 else mixTb

            fw.emit("act", lambda e, dst=dst, mo=mo, mw_=mw_, tb=tb: e.activation(
                dst[0:mw_ - mo, tb * 512:(tb + 1) * 512], B["PS1"][0:mw_ - mo, 0:512],
                AF.Tanh), reads=[B["PS1"]], writes=[dst])
    if "tdbg" in drams:
        fw.emit("sp", lambda e: e.dma_start(out=drams["tdbg"], in_=mixTa[0:96, :]),
                reads=[mixTa], dma=True)

    if L1_STOP == "mixT":
        return
    S5 = {"w": 0, "k": 1, "v": 2, "r": 3, "g": 4}
    for snm in ("g", "v", "k", "w", "r"):
        s5 = S5[snm]
        if s5 < 3:
            mixT, mrow, mk = mixTa, s5 * 32, B["qt2"]
        else:
            mixT, mrow, mk = mixTb, (s5 - 3) * 32, B["kt2"]
        w2c = B["fmt"][mrow:mrow + 32, 0:1024]
        fw.emit("pool", lambda e, w2c=w2c, s5=s5: e.dma_start(out=w2c, in_=ins["TW2_%d" % s5]),
                writes=[B["fmt"]], dma=True)
        maacol, maakey = ccols["MAA" + snm.upper()]
        if snm == "r":
            for k in range(8):
                fw.emit("dve", lambda e, k=k: e.tensor_copy(
                    st[:, k * 3:k * 3 + 3],
                    fm[:, k * T + 511: k * T + 1536: 512]),
                    reads=[fm], writes=[st])
        for tb in range(4):
            for k in range(8):
                fw.emit("pe", lambda e, k=k, mixT=mixT, mrow=mrow, tb=tb, w2c=w2c: e.matmul(
                    B["PS1"][:, 0:512], w2c[:, k * 128:(k + 1) * 128],
                    mixT[mrow:mrow + 32, tb * 512:(tb + 1) * 512],
                    start=True, stop=True),
                    reads=[B["fmt"], mk], writes=[B["PS1"]], pbase=mrow)
                t1 = B["scrA"][:, 0:512]
                fw.emit("dve", lambda e, t1=t1, k=k, maacol=maacol: e.tensor_scalar(
                    t1, B["PS1"][:, 0:512], maacol[:, k:k + 1], None, ALU.add),
                    reads=[B["PS1"], maakey], writes=[B["scrA"]])
                emit_xxs(k, tb, xxs, bnd=(snm == "r"))
                dstm = B["mixbuf"][:, k * 512:(k + 1) * 512]
                fw.emit("dve", lambda e, dstm=dstm, t1=t1, k=k, tb=tb: e.tensor_tensor(
                    dstm, t1, xxs, ALU.mult),
                    reads=[B["scrA"], B["scrB16"]], writes=[B["wgu"]])
                fw.emit("dve", lambda e, dstm=dstm, k=k, tb=tb: e.tensor_tensor(
                    dstm, dstm, fm[:, k * T + tb * 512: k * T + (tb + 1) * 512], ALU.add),
                    reads=[B["wgu"], fm], writes=[B["mixbuf"]])
            if snm == "v" and "mdbg" in drams:
                fw.emit("sp", lambda e, tb=tb: e.dma_start(
                    out=drams["mdbg"][:, tb * 4096:(tb + 1) * 4096],
                    in_=B["mixbuf"][:, :]), reads=[B["wgu"], B["mixbuf"]], dma=True)
            if snm in ("r", "k"):
                dstT = B["RT"] if snm == "r" else B["KT"]
                wname = "WR" if snm == "r" else "WK"
                for oc in range(8):
                    wc = B["wo_c"]
                    src = ins[wname].rearrange("(k p) m -> p k m", p=128)[
                        :, :, oc * 128:(oc + 1) * 128]
                    fw.emit("pool", lambda e, wc=wc, src=src: e.dma_start(
                        out=wc[:, 0:1024].rearrange("p (k m) -> p k m", k=8), in_=src),
                        writes=[B["wo_c"]], dma=True)
                    for k in range(8):
                        fw.emit("pe", lambda e, k=k, wc=wc: e.matmul(
                            B["PS2"][:, 0:512], wc[:, k * 128:(k + 1) * 128],
                            B["mixbuf"][:, k * 512:(k + 1) * 512],
                            start=(k == 0), stop=(k == 7)),
                            reads=[B["wo_c"], B["mixbuf"]], writes=[B["PS2"]])
                    fw.emit("act", lambda e, dstT=dstT, oc=oc, tb=tb: e.activation(
                        dstT[:, oc * T + tb * 512: oc * T + (tb + 1) * 512],
                        B["PS2"][:, 0:512], AF.Copy),
                        reads=[B["PS2"]], writes=[dstT])
            elif snm in ("v", "g"):
                wname = "WV" if snm == "v" else "WG"
                for ttp in range(2):
                    for k in range(8):
                        wc = B["wo_c"]
                        src = ins[wname][k * 128:(k + 1) * 128, :]
                        fw.emit("pool", lambda e, wc=wc, src=src: e.dma_start(
                            out=wc[:, 0:1024], in_=src),
                            writes=[B["wo_c"]], dma=True)
                        for tt in (2 * ttp, 2 * ttp + 1):
                            for nh in range(2):
                                fw.emit("pe", lambda e, k=k, tt=tt, wc=wc, nh=nh: e.matmul(
                                    B["PS2"][:, (tt % 2) * 1024 + nh * 512:
                                             (tt % 2) * 1024 + (nh + 1) * 512],
                                    B["mixbuf"][:, k * 512 + tt * 128: k * 512 + (tt + 1) * 128],
                                    wc[:, nh * 512:(nh + 1) * 512],
                                    start=(k == 0), stop=(k == 7)),
                                    reads=[B["mixbuf"], B["wo_c"]], writes=[B["PS2"]])
                    for tt in (2 * ttp, 2 * ttp + 1):
                        ti = tb * 4 + tt
                        if snm == "v":
                            vt_ = B["vstg"][:, 0:1024]
                            fw.emit("dve", lambda e, vt_=vt_, tt=tt: e.tensor_copy(
                                vt_, B["PS2"][:, (tt % 2) * 1024:(tt % 2 + 1) * 1024]),
                                reads=[B["PS2"]], writes=[B["vstg"]])
                            dstv = drams["vscr"][ti * 128:(ti + 1) * 128, :]
                            fw.emit("sp", lambda e, dstv=dstv, vt_=vt_: e.dma_start(
                                out=dstv, in_=vt_),
                                reads=[B["vstg"]], writes=[("vscr", ti)], dma=True)
                        else:
                            gt = B["scrB16"][:, 0:C]
                            fw.emit("act", lambda e, gt=gt, tt=tt: e.activation(
                                gt, B["PS2"][:, (tt % 2) * 1024:(tt % 2 + 1) * 1024], AF.Silu),
                                reads=[B["PS2"]], writes=[B["scrB16"]])
                            dst = drams["g2scr"][ti * 128:(ti + 1) * 128, :]
                            fw.emit("sp", lambda e, dst=dst, gt=gt: e.dma_start(
                                out=dst, in_=gt),
                                reads=[B["scrB16"]], writes=[("g2scr", ti)], dma=True)
            else:
                for k in range(8):
                    fw.emit("pe", lambda e, k=k: e.matmul(
                        B["PS2"][0:64, 0:512], B["dw1sb"][:, k * 64:(k + 1) * 64],
                        B["mixbuf"][:, k * 512:(k + 1) * 512],
                        start=(k == 0), stop=(k == 7)),
                        reads=[B["dw1sb"], B["mixbuf"]], writes=[B["PS2"]])
                fw.emit("act", lambda e, tb=tb: e.activation(
                    B["twT"][0:64, tb * 512:(tb + 1) * 512], B["PS2"][0:64, 0:512],
                    AF.Tanh), reads=[B["PS2"]], writes=[B["ptt"]])
                for oc in range(8):
                    fw.emit("pe", lambda e, oc=oc, tb=tb: e.matmul(
                        B["PS2"][:, 1024:1536],
                        B["dw2sb"][0:64, oc * 128:(oc + 1) * 128],
                        B["twT"][0:64, tb * 512:(tb + 1) * 512],
                        start=True, stop=True),
                        reads=[B["dw2sb"], B["ptt"]], writes=[B["PS2"]])
                    fw.emit("act", lambda e, oc=oc, tb=tb: e.activation(
                        B["LD"][:, oc * T + tb * 512: oc * T + (tb + 1) * 512],
                        B["PS2"][:, 1024:1536], AF.Exp, bias=ccols["TDEC"][0][:, oc:oc + 1]),
                        reads=[B["PS2"], ccols["TDEC"][1]], writes=[("LD",), B["ot"]])

    if L1_STOP == "mixes":
        return
    def dump_rkld():
        if "rdbg" not in drams:
            return
        for nm_, buf, bk_ in (("rdbg", B["RT"], B["fm"]), ("kdbg", B["KT"], B["KT"]),
                              ("lddbg", B["LD"], B["ot"])):
            fw.emit("sp", lambda e, nm_=nm_, buf=buf: e.dma_start(
                out=drams[nm_], in_=buf[:, :]), reads=[bk_, ("LD",)],
                writes=[(nm_, "D")], dma=True)
    dump_rkld()
    ucol, uk = ccols["UCOL"]
    for k in range(8):
        pch = B["scrB16"][:, 0:2048]
        fw.emit("dve", lambda e, pch=pch, k=k: e.scalar_tensor_tensor(
            pch, B["RT"][:, k * T:(k + 1) * T], ucol[:, k:k + 1],
            B["KT"][:, k * T:(k + 1) * T], ALU.mult, ALU.mult),
            reads=[B["RT"], B["KT"], uk], writes=[B["scrB16"]])
        for nb in range(4):
            fw.emit("pe", lambda e, pch=pch, k=k, nb=nb: e.matmul(
                B["PS1"][0:16, nb * 512:(nb + 1) * 512],
                B["indic"][:, k * 16:(k + 1) * 16],
                pch[:, nb * 512:(nb + 1) * 512],
                start=(k == 0), stop=(k == 7)),
                reads=[B["indic"], B["scrB16"]], writes=[B["PS1"]])
    bon = B["bonF"][0:16, 0:2048]
    fw.emit("dve", lambda e, bon=bon: e.tensor_copy(bon, B["PS1"][0:16, 0:2048]),
            reads=[B["PS1"]], writes=[B["xstg"]])
    for ti in range(NT):
        fw.emit("pe", lambda e, bon=bon, ti=ti: e.transpose(
            B["psT"][0:128, 1024:1040], bon[:, ti * 128:(ti + 1) * 128],
            B["idf"][0:16, 0:16]),
            reads=[B["xstg"], B["idf"]], writes=[B["PS1"]])
        fw.emit("dve", lambda e, ti=ti: e.tensor_copy(
            B["bonT"][:, ti * 16:(ti + 1) * 16], B["psT"][0:128, 1024:1040]),
            reads=[B["PS1"]], writes=[B["bonT"]])

    if L1_STOP == "bonus":
        return
    fw.emit("dve", lambda e: e.memset(B["Sbig"][:, :], 0.0), writes=[B["Sbig"]])
    fw.emit("dve", lambda e: e.memset(B["Sb16"][:, :], 0.0), writes=[B["Sb16"]])
    fw.emit("dve", lambda e: e.memset(B["ones64"][:, :], 1.0), writes=[B["ones64"]])
    LDv = B["LD"][:, :].rearrange("p (k t) -> p k t", k=8)
    Cc = B["qsb"]["dqe"][:, 0:512]
    Ccv = Cc.rearrange("p (k t) -> p k t", k=8)
    Ce = B["qsb"]["rdqe"][:, 0:512]
    EA = B["v2a"][:, 0:512]
    EB = B["v2a"][:, 512:1024]
    EB2 = B["v2a"][:, 1024:1536]
    rtc = B["ptt"][:, 0:512]
    ktc = B["ptt"][:, 512:1024]
    kbf = B["qsb"]["q4t"][:, 1024:1536]
    edec = B["cols"][:, 0:8]
    kbar = B["qt2"][:, 0:1024]
    Am = B["kt2"][:, 0:1024]
    psA2 = B["PS1"][0:64, 0:1024]
    psY = B["PS2"][:, 0:1024]
    psS = B["PS2"][:, 1024:1536]
    ystage = B["scrB16"][:, 0:1024]
    v3 = lambda ap: ap.rearrange("p (k t) -> p k t", k=8)
    for c in range(T // 64):
        ti, par = c // 2, (c % 2) * 64
        if par == 0:
            vsrc = drams["vscr"][ti * 128:(ti + 1) * 128, :]
            fw.emit("sp", lambda e, vsrc=vsrc: e.dma_start(out=B["vstg"][:, :], in_=vsrc),
                    reads=[("vscr", ti)], writes=[B["vstg"]], dma=True)
        for k in range(8):
            fw.emit("dve", lambda e, k=k, c=c: e.tensor_tensor_scan(
                Cc[:, k * 64:(k + 1) * 64], B["ones64"][:, :],
                B["LD"][:, k * T + c * 64: k * T + (c + 1) * 64],
                0.0, ALU.mult, ALU.add),
                reads=[("LD",), B["ones64"]], writes=[B["qsb"]["dqe"]])
        if L1_STOP == "sc_cc%d" % c:
            return
        fw.emit("dve", lambda e, c=c: e.tensor_tensor(
            v3(Ce), Ccv, LDv[:, :, c * 64:(c + 1) * 64], ALU.subtract),
            reads=[B["qsb"]["dqe"], ("LD",)], writes=[B["qsb"]["rdqe"]])
        fw.emit("act", lambda e: e.activation(EA, Ce, AF.Exp, scale=-1.0),
                reads=[B["qsb"]["rdqe"]], writes=[B["v2a"]])
        fw.emit("act", lambda e: e.activation(EB, Cc, AF.Exp),
                reads=[B["qsb"]["dqe"]], writes=[B["v2a"]])
        t2 = B["qsb"]["zz"][:, 0:512]
        fw.emit("dve", lambda e, t2=t2: e.tensor_tensor(
            v3(t2), Ccv, Ccv[:, :, 63:64].broadcast_to([128, 8, 64]), ALU.subtract),
            reads=[B["qsb"]["dqe"]], writes=[B["qsb"]["zz"]])
        fw.emit("act", lambda e, t2=t2: e.activation(EB2, t2, AF.Exp),
                reads=[B["qsb"]["zz"]], writes=[B["v2a"]])
        fw.emit("act", lambda e: e.activation(edec, Cc[:, 63:512:64], AF.Exp, scale=-1.0),
                reads=[B["qsb"]["dqe"]], writes=[B["cols"]])
        if L1_STOP == "sc_exp%d" % c:
            return
        fw.emit("dve", lambda e, c=c: e.tensor_tensor(
            v3(rtc), v3(B["RT"][:, :])[:, :, c * 64:(c + 1) * 64]
            if False else B["RT"][:, :].rearrange("p (k t) -> p k t", k=8)[:, :, c * 64:(c + 1) * 64],
            v3(EA), ALU.mult),
            reads=[B["RT"], B["v2a"]], writes=[B["ptt"]])
        fw.emit("dve", lambda e, c=c: e.tensor_tensor(
            v3(ktc), B["KT"][:, :].rearrange("p (k t) -> p k t", k=8)[:, :, c * 64:(c + 1) * 64],
            v3(EB), ALU.mult),
            reads=[B["KT"], B["v2a"]], writes=[B["ptt"]])
        fw.emit("dve", lambda e, c=c: e.tensor_tensor(
            v3(kbf), B["KT"][:, :].rearrange("p (k t) -> p k t", k=8)[:, :, c * 64:(c + 1) * 64],
            v3(EB2), ALU.mult),
            reads=[B["KT"], B["v2a"]], writes=[B["qsb"]["q4t"]])
        for k in range(8):
            fw.emit("pe", lambda e, k=k: e.transpose(
                B["psT"][0:64, 1024:1152], kbf[:, k * 64:(k + 1) * 64], B["idf"][:, :]),
                reads=[B["qsb"]["q4t"], B["idf"]], writes=[B["PS1"]])
            fw.emit("dve", lambda e, k=k, par=par: e.tensor_copy(
                kbar[par:par + 64, k * 128:(k + 1) * 128], B["psT"][0:64, 1024:1152]),
                reads=[B["PS1"]], writes=[B["qt2"]])
        if L1_STOP == "sc_kb%d" % c:
            if "mdbg" in drams:
                fw.emit("sp", lambda e: e.dma_start(
                    out=drams["mdbg"][:, 0:1024], in_=kbar[:, :]),
                    reads=[B["qt2"]], writes=[("mdbgD",)], dma=True)
                fw.emit("sp", lambda e: e.dma_start(
                    out=drams["mdbg"][:, 1024:2048], in_=B["vstg"][:, :]),
                    reads=[B["vstg"]], writes=[("mdbgD",)], dma=True)
            return
        for h in ([] if "A" in L1_SKIP else [0, 2, 4, 6, 8, 10, 12, 14, 1, 3, 5, 7, 9, 11, 13, 15]):
            rb = (h % 2) * 64
            cb = (h // 2) * 64
            fw.emit("pe", lambda e, h=h, rb=rb, cb=cb: e.matmul(
                psA2[:, h * 64:(h + 1) * 64],
                ktc[rb:rb + 64, cb:cb + 64], rtc[rb:rb + 64, cb:cb + 64],
                start=(h in (0, 8)), stop=(h in (7, 15))),
                reads=[B["ptt"], B["ptt"]], writes=[B["PS1"]], pbase=rb)
        if L1_STOP == "sc_mm%d" % c:
            return
        fw.emit("dve", lambda e, par=par: e.tensor_tensor(
            Am[par:par + 64, :], psA2[:, :], B["umask"][0:64, :], ALU.mult),
            reads=[B["PS1"], B["umask"]], writes=[B["kt2"]])
        if L1_STOP == "sc_A%d" % c:
            return
        for h in ([] if "y1" in L1_SKIP else range(16)):
            fw.emit("pe", lambda e, h=h, par=par, ti=ti: e.matmul(
                psY[par:par + 64, h * 64:(h + 1) * 64],
                Am[par:par + 64, h * 64:(h + 1) * 64],
                B["vstg"][par:par + 64, h * 64:(h + 1) * 64],
                start=(h % 8 == 0), stop=False),
                reads=[B["kt2"], B["vstg"]], writes=[B["PS2"]], pbase=par, obase=par)
        for h in ([] if "y2" in L1_SKIP else [0, 2, 4, 6, 8, 10, 12, 14, 1, 3, 5, 7, 9, 11, 13, 15]):
            rb = (h % 2) * 64
            cb = (h // 2) * 64
            fw.emit("pe", lambda e, h=h, rb=rb, cb=cb, par=par: e.matmul(
                psY[par:par + 64, h * 64:(h + 1) * 64],
                rtc[rb:rb + 64, cb:cb + 64],
                B["Sb16"][rb:rb + 64, cb:cb + 64],
                start=False, stop=(h in (7, 15))),
                reads=[B["ptt"], B["Sb16"]], writes=[B["PS2"]], pbase=rb, obase=par)
        bv = B["qt2"][:, 1024:2048]
        fw.emit("dve", lambda e, bv=bv, par=par, ti=ti: e.tensor_tensor(
            bv[par:par + 64, :].rearrange("p (h d) -> p h d", h=16),
            B["vstg"][par:par + 64, 0:1024].rearrange("p (h d) -> p h d", h=16),
            B["bonT"][par:par + 64, ti * 16:(ti + 1) * 16]
            .unsqueeze(2).broadcast_to([64, 16, 64]), ALU.mult),
            reads=[B["vstg"], B["bonT"]], writes=[B["qt2"]])
        fw.emit("dve", lambda e, bv=bv, par=par: e.tensor_tensor(
            ystage[par:par + 64, :], psY[par:par + 64, :], bv[par:par + 64, :], ALU.add),
            reads=[B["PS2"], B["qt2"]], writes=[B["scrB16"]])
        if par == 64:
            dst = drams["yscr"][ti * 128:(ti + 1) * 128, :]
            fw.emit("sp", lambda e, dst=dst: e.dma_start(out=dst, in_=ystage),
                    reads=[B["scrB16"]], writes=[("yscr", ti)], dma=True)
        if L1_STOP == "sc_y%d" % c:
            return
        for hi, h in enumerate(range(16)):
            rb = (h % 2) * 64
            cb = (h // 2) * 64
            fw.emit("pe", lambda e, h=h, rb=rb, cb=cb, par=par, ti=ti: e.matmul(
                psS[rb:rb + 64, cb:cb + 64],
                kbar[par:par + 64, h * 64:(h + 1) * 64],
                B["vstg"][par:par + 64, h * 64:(h + 1) * 64],
                start=(h in (0, 1)), stop=(h == 15)),
                reads=[B["qt2"], B["vstg"]], writes=[B["PS2"]], pbase=par, obase=rb)
        if L1_STOP == "sc_P%d" % c:
            if "sdump" in drams:
                pst = B["scrA"][:, 0:512]
                fw.emit("dve", lambda e, pst=pst: e.tensor_copy(pst, psS[:, :]),
                        reads=[B["PS2"]], writes=[B["scrA"]])
                fw.emit("sp", lambda e, pst=pst: e.dma_start(out=drams["sdump"], in_=pst),
                        reads=[B["scrA"]], writes=[("sdumpD",)], dma=True)
            dump_rkld()
            return
        for p in range(8):
            fw.emit("dve", lambda e, p=p: e.scalar_tensor_tensor(
                B["Sbig"][:, p * 64:(p + 1) * 64], B["Sbig"][:, p * 64:(p + 1) * 64],
                edec[:, p:p + 1], psS[:, p * 64:(p + 1) * 64], ALU.mult, ALU.add),
                reads=[B["Sbig"], B["cols"], B["PS2"]], writes=[B["Sbig"]])
        fw.emit("dve", lambda e: e.tensor_copy(B["Sb16"][:, :], B["Sbig"][:, :]),
                reads=[B["Sbig"]], writes=[B["Sb16"]])
        if L1_STOP == "sc_S%d" % c:
            if "sdump" in drams:
                fw.emit("sp", lambda e: e.dma_start(out=drams["sdump"], in_=B["Sbig"][:, :]),
                        reads=[B["Sbig"]], writes=[("sdumpD",)], dma=True)
            dump_rkld()
            return

    if L1_STOP == "scan":
        dump_rkld()
        return
    lnwb = B["wgu"][:, 0:1024]
    lnbb = B["wgu"][:, 1024:2048]
    bcast_dma(fw, B, ins["LNW"][0:1, :], lnwb, B["wgu"], C, cast=True)
    bcast_dma(fw, B, ins["LNB"][0:1, :], lnbb, B["wgu"], C, cast=True)
    LN_EPS = 1e-5 * 64
    for ti in range(NT):
        yt = B["scrB16"][:, 0:1024]
        src = drams["yscr"][ti * 128:(ti + 1) * 128, :]
        fw.emit("sp", lambda e, yt=yt, src=src: e.dma_start(out=yt, in_=src),
                reads=[("yscr", ti)], writes=[B["scrB16"]], dma=True)
        g2 = B["scrB16"][:, 1024:2048]
        src = drams["g2scr"][ti * 128:(ti + 1) * 128, :]
        fw.emit("sp", lambda e, g2=g2, src=src: e.dma_start(out=g2, in_=src),
                reads=[("g2scr", ti)], writes=[B["scrB16"]], dma=True)
        if ti == 0 and L1_STOP == "gna":
            dump_rkld()
            return
        ytv = yt.rearrange("p (h d) -> p h d", h=16)
        sY = st[:, 0:16]
        fw.emit("dve", lambda e, ytv=ytv, sY=sY: e.tensor_reduce(
            sY.unsqueeze(2), ytv, AX.X, ALU.add),
            reads=[B["scrB16"]], writes=[st])
        sq = B["qsb"]["q4t"][:, 0:1024]
        fw.emit("act", lambda e, yt=yt, sq=sq: e.activation(sq, yt, AF.Square),
                reads=[B["scrB16"]], writes=[B["qsb"]["q4t"]])
        sY2 = st[:, 16:32]
        fw.emit("dve", lambda e, sq=sq, sY2=sY2: e.tensor_reduce(
            sY2.unsqueeze(2), sq.rearrange("p (h d) -> p h d", h=16), AX.X, ALU.add),
            reads=[B["qsb"]["q4t"]], writes=[st])
        muH = st[:, 32:48]
        fw.emit("dve", lambda e, muH=muH, sY=sY: e.tensor_scalar(
            muH, sY, 1.0 / 64, None, ALU.mult), reads=[st], writes=[st])
        m2H = st[:, 48:64]
        fw.emit("dve", lambda e, m2H=m2H, sY2=sY2: e.tensor_scalar(
            m2H, sY2, 1.0 / 64, None, ALU.mult), reads=[st], writes=[st])
        vH = st[:, 64:80]
        fw.emit("dve", lambda e, vH=vH, muH=muH: e.tensor_tensor(vH, muH, muH, ALU.mult),
                reads=[st], writes=[st])
        fw.emit("dve", lambda e, vH=vH, m2H=m2H: e.tensor_tensor(vH, m2H, vH, ALU.subtract),
                reads=[st], writes=[st])
        fw.emit("dve", lambda e, vH=vH: e.tensor_scalar(vH, vH, LN_EPS, None, ALU.add),
                reads=[st], writes=[st])
        rH = st[:, 80:96]
        fw.emit("dve", lambda e, rH=rH, vH=vH: e.reciprocal(rH, vH), reads=[st], writes=[st])
        fw.emit("act", lambda e, rH=rH: e.activation(rH, rH, AF.Sqrt), reads=[st], writes=[st])
        if ti == 0 and L1_STOP == "gnb":
            dump_rkld()
            return
        z1 = B["scrA"][:, 0:1024]
        fw.emit("dve", lambda e, z1=z1, ytv=ytv, muH=muH: e.tensor_tensor(
            z1.rearrange("p (h d) -> p h d", h=16), ytv,
            muH.unsqueeze(2).broadcast_to([128, 16, 64]), ALU.subtract),
            reads=[B["scrB16"], st], writes=[B["scrA"]])
        fw.emit("dve", lambda e, z1=z1, rH=rH: e.tensor_tensor(
            z1.rearrange("p (h d) -> p h d", h=16),
            z1.rearrange("p (h d) -> p h d", h=16),
            rH.unsqueeze(2).broadcast_to([128, 16, 64]), ALU.mult),
            reads=[B["scrA"], st], writes=[B["scrA"]])
        fw.emit("dve", lambda e, z1=z1: e.tensor_tensor(z1, z1, lnwb, ALU.mult),
                reads=[B["scrA"], B["wgu"]], writes=[B["scrA"]])
        fw.emit("dve", lambda e, z1=z1: e.tensor_tensor(z1, z1, lnbb, ALU.add),
                reads=[B["scrA"], B["wgu"]], writes=[B["scrA"]])
        fw.emit("dve", lambda e, z1=z1, g2=g2: e.tensor_tensor(z1, z1, g2, ALU.mult),
                reads=[B["scrA"], B["scrB16"]], writes=[B["scrA"]])
        if ti == 0 and L1_STOP == "gnc":
            dump_rkld()
            return
        for k in range(8):
            fw.emit("pe", lambda e, z1=z1, k=k: e.transpose(
                psT, z1[:, k * 128:(k + 1) * 128], B["idf"][:, :]),
                reads=[B["scrA"], B["idf"]], writes=[B["PS1"]])
            dstf = B["fmt"][:, k * 128:(k + 1) * 128]
            fw.emit("dve", lambda e, dstf=dstf: e.tensor_copy(dstf, psT),
                    reads=[B["PS1"]], writes=[B["fmt"]])
        for k in range(8):
            wc = B["wo_c"]
            src = ins["WO1"][k * 128:(k + 1) * 128, :]
            fw.emit("pool", lambda e, wc=wc, src=src: e.dma_start(out=wc[:, 0:1024], in_=src),
                    writes=[B["wo_c"]], dma=True)
            lhs = B["fmt"][:, k * 128:(k + 1) * 128]
            for nh in range(2):
                fw.emit("pe", lambda e, lhs=lhs, nh=nh, k=k, wc=wc: e.matmul(
                    B["psD"][:, nh * 512:(nh + 1) * 512], lhs,
                    wc[:, nh * 512:(nh + 1) * 512],
                    start=(k == 0), stop=(k == 7)),
                    reads=[B["fmt"], B["wo_c"]], writes=[B["PS2"]])
        if ti == 0 and L1_STOP == "gnd":
            dump_rkld()
            return
        gt = B["scrB16"][:, 0:C]
        src = drams["gscr"][ti * 128:(ti + 1) * 128, :]
        fw.emit("sp", lambda e, gt=gt, src=src: e.dma_start(out=gt, in_=src),
                reads=[("gscr", ti)], writes=[B["scrB16"]], dma=True)
        s1 = B["scrA"][:, 0:C]
        fw.emit("dve", lambda e, s1=s1, gt=gt: e.tensor_tensor(
            s1, B["psD"][:, 0:1024], gt, ALU.mult),
            reads=[B["PS2"], B["scrB16"]], writes=[B["scrA"]])
        xt, xk = ld_xtile(fw, B, xdram, ti)
        fw.emit("dve", lambda e, xt=xt, s1=s1: e.tensor_tensor(xt, xt, s1, ALU.add),
                reads=[B["scrA"], xk], writes=[xk])
        st_xtile(fw, B, xdram, ti, xt, [xk])
        if L1_STOP == "gn%d" % ti:
            dump_rkld()
            return
    dump_rkld()


def emit_imp(fw, B, drams, IMP):
    st = B["stats"]
    negmu = st[:, 96:112]
    rstd = st[:, 112:128]
    emit_ln(fw, B, drams["x"], negmu, rstd)
    v = st[:, 16:32]
    fw.emit("dve", lambda e: e.tensor_tensor(v, st[:, 0:16], st[:, 80:96], ALU.mult),
            reads=[st], writes=[st])
    fw.emit("dve", lambda e: e.tensor_scalar(v, v, float(C), None, ALU.mult),
            reads=[st], writes=[st])
    fw.emit("dve", lambda e: e.tensor_scalar(v, v, 0.0, None, ALU.max),
            reads=[st], writes=[st])
    fw.emit("act", lambda e: e.activation(v, v, AF.Sqrt), reads=[st], writes=[st])
    fw.emit("sp", lambda e: e.dma_start(out=IMP, in_=v), reads=[st],
            writes=[("IMP",)], dma=True)



def emit_l2(fw, B, ins, drams):
    st = B["stats"]
    negmu = st[:, 96:112]
    rstd = st[:, 112:128]
    fm = B["fm"]
    xdram = drams["x"]
    psT = B["psT"][:, 1024:1152]
    emit_ln(fw, B, xdram, negmu, rstd)
    emit_fm_build(fw, B, xdram, fm, negmu, rstd)

    FMQ = B["ot"]
    FMK = B["KT"]
    wguf = B["wgu"][:, :].bitcast(F32)
    wgu2f = B["wgu2"][:, :].bitcast(F32)
    wdf = B["wd"][:, :].bitcast(F32)

    fw.emit("pool", lambda e: e.dma_start(out=B["w1sb"][:, 0:256], in_=ins["WPJI"]),
            writes=[B["w1sb"]], dma=True)
    fw.emit("sp", lambda e: e.dma_start(out=B["qsb"]["dq"][:, 0:8], in_=ins["BQC"]),
            writes=[B["qsb"]["dq"]], dma=True)
    fw.emit("sp", lambda e: e.dma_start(out=B["qsb"]["dq"][:, 8:16], in_=ins["BKC"]),
            writes=[B["qsb"]["dq"]], dma=True)
    fw.emit("sp", lambda e: e.dma_start(out=st[0:1, 0:16], in_=ins["LBW"]),
            writes=[st], dma=True)

    for dstT, dstk, wname, bcol in ((FMQ, B["ot"], "QW2", 0), (FMK, B["KT"], "KW2", 8)):
        for oc in range(8):
            wc = B["wo_c"]
            src = ins[wname].rearrange("(k p) m -> p k m", p=128)[
                :, :, oc * 128:(oc + 1) * 128]
            fw.emit("pool", lambda e, wc=wc, src=src: e.dma_start(
                out=wc[:, 0:1024].rearrange("p (k m) -> p k m", k=8), in_=src),
                writes=[B["wo_c"]], dma=True)
            for tb in range(4):
                for k in range(8):
                    fw.emit("pe", lambda e, k=k, wc=wc, tb=tb: e.matmul(
                        B["PS2"][:, 0:512], wc[:, k * 128:(k + 1) * 128],
                        fm[:, k * T + tb * 512:(k * T) + (tb + 1) * 512],
                        start=(k == 0), stop=(k == 7)),
                        reads=[B["wo_c"], fm], writes=[B["PS2"]])
                fw.emit("dve", lambda e, dstT=dstT, oc=oc, tb=tb, bcol=bcol: e.tensor_scalar(
                    dstT[:, oc * T + tb * 512: oc * T + (tb + 1) * 512],
                    B["PS2"][:, 0:512], B["qsb"]["dq"][:, bcol + oc:bcol + oc + 1],
                    None, ALU.add),
                    reads=[B["PS2"], B["qsb"]["dq"]], writes=[dstk])

    lbw = st[0:1, 0:16]
    fw.emit("act", lambda e: e.activation(lbw, lbw, AF.Exp), reads=[st], writes=[st])
    fw.emit("dve", lambda e: e.tensor_scalar(lbw, lbw, 1.0, None, ALU.add),
            reads=[st], writes=[st])
    fw.emit("act", lambda e: e.activation(lbw, lbw, AF.Ln), reads=[st], writes=[st])
    fw.emit("dve", lambda e: e.tensor_scalar(lbw, lbw, 1e-4, None, ALU.add),
            reads=[st], writes=[st])
    rbw = st[0:1, 16:32]
    fw.emit("dve", lambda e: e.reciprocal(rbw, lbw), reads=[st], writes=[st])
    fw.emit("pe", lambda e: e.matmul(B["PS1"][:, 1536:1552], B["ones1"][:, :], rbw,
                                     start=True, stop=True),
            reads=[st, B["ones1"]], writes=[B["PS1"]])
    RBWB = st[:, 16:32]
    fw.emit("dve", lambda e: e.tensor_copy(RBWB, B["PS1"][:, 1536:1552]),
            reads=[B["PS1"]], writes=[st])
    fw.emit("pe", lambda e: e.transpose(B["PS1"][0:16, 1024:1025], rbw,
                                        B["idf"][0:1, 0:1]),
            reads=[st, B["idf"]], writes=[B["PS1"]])
    rbwcol = st[0:16, 32:33]
    fw.emit("dve", lambda e: e.tensor_copy(rbwcol, B["PS1"][0:16, 1024:1025]),
            reads=[B["PS1"]], writes=[st])

    zqTs = wgu2f[0:16, 0:2048]
    negzk = B["qsb"]["dqe"][:, 0:256]
    psA = B["PS1"][:, 1536:1552]
    zsb = B["qsb"]["dqe"][:, 256:272]
    for src, srck, isq in ((FMQ, B["ot"], True), (FMK, B["KT"], False)):
        for h in range(16):
            rb = (h % 2) * 64
            kg = h // 2
            for j in range(16):
                fw.emit("pe", lambda e, src=src, rb=rb, kg=kg, j=j: e.matmul(
                    psA, src[rb:rb + 64, kg * T + j: (kg + 1) * T: 16],
                    B["w1sb"][rb:rb + 64, j * 16:(j + 1) * 16],
                    start=(j == 0), stop=(j == 15)),
                    reads=[srck, B["w1sb"]], writes=[B["PS1"]], pbase=rb)
            if isq:
                fw.emit("dve", lambda e: e.tensor_copy(zsb, psA),
                        reads=[B["PS1"]], writes=[B["qsb"]["dqe"]])
                fw.emit("pe", lambda e: e.transpose(
                    B["PS1"][0:16, 1024:1152], zsb, B["idf"][:, :]),
                    reads=[B["qsb"]["dqe"], B["idf"]], writes=[B["PS1"]])
                fw.emit("dve", lambda e, h=h: e.tensor_scalar(
                    zqTs[:, h * 128:(h + 1) * 128], B["PS1"][0:16, 1024:1152],
                    rbwcol, None, ALU.mult),
                    reads=[B["PS1"], st], writes=[B["wgu2"]])
            else:
                fw.emit("dve", lambda e, h=h: e.scalar_tensor_tensor(
                    negzk[:, h * 16:(h + 1) * 16], psA, -1.0, RBWB,
                    ALU.mult, ALU.mult),
                    reads=[B["PS1"], st], writes=[B["qsb"]["dqe"]])
    fw.emit("sp", lambda e: e.dma_start(out=drams["zqT"], in_=zqTs),
            reads=[B["wgu2"]], writes=[("zqT",)], dma=True)

    BVB = wguf[:, 0:1024]
    fw.emit("sp", lambda e: e.dma_start(
        out=BVB, in_=ins["BVR"][0:1, :].broadcast_to([128, 1024])),
        writes=[B["wgu"]], dma=True)
    TMV = B["ot"]
    for tb in range(4):
        for ttp in range(2):
            for k in range(8):
                wc = B["wo_c"]
                src = ins["VW2"][k * 128:(k + 1) * 128, :]
                fw.emit("pool", lambda e, wc=wc, src=src: e.dma_start(
                    out=wc[:, 0:1024], in_=src),
                    writes=[B["wo_c"]], dma=True)
                for tt in (2 * ttp, 2 * ttp + 1):
                    it = tb * 4 + tt
                    for nh in range(2):
                        fw.emit("pe", lambda e, k=k, it=it, tt=tt, nh=nh, wc=wc: e.matmul(
                            B["PS2"][:, (tt % 2) * 1024 + nh * 512:
                                     (tt % 2) * 1024 + (nh + 1) * 512],
                            fm[:, k * T + it * 128: k * T + (it + 1) * 128],
                            wc[:, nh * 512:(nh + 1) * 512],
                            start=(k == 0), stop=(k == 7)),
                            reads=[fm, B["wo_c"]], writes=[B["PS2"]])
            for tt in (2 * ttp, 2 * ttp + 1):
                it = tb * 4 + tt
                fw.emit("dve", lambda e, it=it, tt=tt: e.tensor_tensor(
                    TMV[:, it * 1024:(it + 1) * 1024],
                    B["PS2"][:, (tt % 2) * 1024:(tt % 2 + 1) * 1024],
                    BVB, ALU.add),
                    reads=[B["PS2"], B["wgu"]], writes=[B["ot"]])

    ZQB = wguf[:, 0:2048]
    Wt = B["scrB16"][:, 0:2048]
    for p in range(16):
        fw.emit("sp", lambda e, p=p: e.dma_start(
            out=ZQB, in_=drams["zqT"][p:p + 1, :].broadcast_to([128, 2048])),
            reads=[("zqT",)], writes=[B["wgu"]], dma=True)
        for jb in range(16):
            nzc = negzk[:, jb * 16 + p: jb * 16 + p + 1]
            for hf in range(2):
                u = B["scrA"][:, 0:1024]
                fw.emit("act", lambda e, u=u, hf=hf, nzc=nzc: e.activation(
                    u, ZQB[:, hf * 1024:(hf + 1) * 1024], AF.Abs, bias=nzc),
                    reads=[B["wgu"], B["qsb"]["dqe"]], writes=[B["scrA"]])
                fw.emit("act", lambda e, u=u, hf=hf: e.activation(
                    Wt[:, hf * 1024:(hf + 1) * 1024], u, AF.Relu,
                    bias=1.0, scale=-1.0),
                    reads=[B["scrA"]], writes=[B["scrB16"]])
            for it in range(16):
                fw.emit("pe", lambda e, it=it, jb=jb, p=p: e.matmul(
                    B["PS2"][:, it * 64:(it + 1) * 64],
                    Wt[:, it * 128:(it + 1) * 128],
                    TMV[:, jb * 1024 + p * 64: jb * 1024 + (p + 1) * 64],
                    start=(jb == 0 and it in (0, 8)),
                    stop=(jb == 15 and it in (7, 15))),
                    reads=[B["scrB16"], B["ot"]], writes=[B["PS2"]])
        for it in range(16):
            fw.emit("act", lambda e, it=it: e.activation(
                B["vstg"][:, it * 64:(it + 1) * 64],
                B["PS2"][:, it * 64:(it + 1) * 64], AF.Copy, scale=1.0 / T),
                reads=[B["PS2"]], writes=[B["vstg"]])
        dst = drams["yscr"][:, p * 64:(p + 1) * 64].rearrange(
            "(it q) d -> q it d", q=128)
        fw.emit("sp", lambda e, dst=dst, p=p: e.dma_start(
            out=dst, in_=B["vstg"][:, 0:1024].rearrange("q (it d) -> q it d", d=64)),
            reads=[B["vstg"]], writes=[("yscr", p)], dma=True)

    rmax = wguf[:, 0:1024]
    fw.emit("dve", lambda e: e.memset(rmax, 0.0), writes=[B["wgu"]])
    for kb in range(8):
        t = B["scrA"][:, 0:1024]
        src = ins["RHS"][kb * 128:(kb + 1) * 128, :]
        fw.emit("sp", lambda e, t=t, src=src: e.dma_start(out=t, in_=src),
                writes=[B["scrA"]], dma=True)
        fw.emit("act", lambda e, t=t: e.activation(t, t, AF.Abs),
                reads=[B["scrA"]], writes=[B["scrA"]])
        fw.emit("dve", lambda e, t=t: e.tensor_tensor(rmax, rmax, t, ALU.max),
                reads=[B["wgu"], B["scrA"]], writes=[B["wgu"]])
    rsc = B["qsb"]["sc"]
    for nb in range(8):
        fw.emit("pe", lambda e, nb=nb: e.transpose(
            B["PS1"][:, 1024:1152], rmax[:, nb * 128:(nb + 1) * 128], B["idf"][:, :]),
            reads=[B["wgu"], B["idf"]], writes=[B["PS1"]])
        fw.emit("dve", lambda e, nb=nb: e.tensor_reduce(
            rsc[:, nb:nb + 1], B["PS1"][:, 1024:1152], AX.X, ALU.max),
            reads=[B["PS1"]], writes=[B["qsb"]["sc"]])
    fw.emit("dve", lambda e: e.tensor_scalar(rsc[:, 0:8], rsc[:, 0:8], 1e-12,
                                             None, ALU.max),
            reads=[B["qsb"]["sc"]], writes=[B["qsb"]["sc"]])
    fw.emit("dve", lambda e: e.reciprocal(rsc[:, 8:16], rsc[:, 0:8]),
            reads=[B["qsb"]["sc"]], writes=[B["qsb"]["sc"]])
    fw.emit("dve", lambda e: e.tensor_scalar(rsc[:, 8:16], rsc[:, 8:16], 127.0,
                                             None, ALU.mult),
            reads=[B["qsb"]["sc"]], writes=[B["qsb"]["sc"]])
    fw.emit("dve", lambda e: e.tensor_scalar(rsc[:, 16:24], rsc[:, 0:8], 1.0 / 127.0,
                                             None, ALU.mult),
            reads=[B["qsb"]["sc"]], writes=[B["qsb"]["sc"]])
    rowf = B["qsb"]["absx"][0:1, 0:1024]
    RECRSB = wguf[:, 1024:2048]
    RSB = wdf[:, 0:1024]
    for which, cc0, dstb, dkey in ((0, 8, RECRSB, B["wgu"]), (1, 16, RSB, B["wd"])):
        for nb in range(8):
            fw.emit("pe", lambda e, nb=nb, cc0=cc0: e.transpose(
                B["PS1"][0:1, 1024:1152], rsc[:, cc0 + nb:cc0 + nb + 1],
                B["idf"][:, :]),
                reads=[B["qsb"]["sc"], B["idf"]], writes=[B["PS1"]])
            fw.emit("dve", lambda e, nb=nb: e.tensor_copy(
                rowf[:, nb * 128:(nb + 1) * 128], B["PS1"][0:1, 1024:1152]),
                reads=[B["PS1"]], writes=[B["qsb"]["absx"]])
        for nb in range(8):
            fw.emit("pe", lambda e, nb=nb: e.matmul(
                B["PS1"][:, 0:128], B["ones1"][:, :],
                rowf[:, nb * 128:(nb + 1) * 128], start=True, stop=True),
                reads=[B["qsb"]["absx"], B["ones1"]], writes=[B["PS1"]])
            fw.emit("dve", lambda e, nb=nb, dstb=dstb, dkey=dkey: e.tensor_copy(
                dstb[:, nb * 128:(nb + 1) * 128], B["PS1"][:, 0:128]),
                reads=[B["PS1"]], writes=[dkey])
    qrB = B["KT"]
    for kb in range(8):
        t = B["scrA"][:, 0:1024]
        src = ins["RHS"][kb * 128:(kb + 1) * 128, :]
        fw.emit("sp", lambda e, t=t, src=src: e.dma_start(out=t, in_=src),
                writes=[B["scrA"]], dma=True)
        fw.emit("dve", lambda e, t=t: e.tensor_tensor(t, t, RECRSB, ALU.mult),
                reads=[B["scrA"], B["wgu"]], writes=[B["scrA"]])
        fw.emit("dve", lambda e, t=t: e.tensor_scalar(t, t, 1.0, RC, ALU.mult, ALU.add),
                reads=[B["scrA"]], writes=[B["scrA"]])
        fw.emit("dve", lambda e, t=t: e.tensor_scalar(t, t, -RC, None, ALU.add),
                reads=[B["scrA"]], writes=[B["scrA"]])
        fw.emit("dve", lambda e, t=t: e.tensor_scalar(t, t, 127.0, -127.0,
                                                      ALU.min, ALU.max),
                reads=[B["scrA"]], writes=[B["scrA"]])
        fw.emit("dve", lambda e, t=t, kb=kb: e.tensor_copy(
            qrB[:, kb * 1024:(kb + 1) * 1024], t),
            reads=[B["scrA"]], writes=[B["KT"]])
    for ti in range(NT):
        xt, xk = ld_xtile(fw, B, xdram, ti)
        xn = B["scrA"][:, 0:1024]
        fw.emit("dve", lambda e, xn=xn, xt=xt, ti=ti: e.tensor_scalar(
            xn, xt, negmu[:, ti:ti + 1], rstd[:, ti:ti + 1], ALU.add, ALU.mult),
            reads=[xk, st], writes=[B["scrA"]])
        acol = st[:, 0:1]
        fw.emit("dve", lambda e, xn=xn, acol=acol: e.tensor_reduce(
            acol, xn, AX.X, ALU.max, apply_absolute_value=True),
            reads=[B["scrA"]], writes=[st])
        fw.emit("dve", lambda e, acol=acol: e.tensor_scalar(
            acol, acol, 1e-12, None, ALU.max), reads=[st], writes=[st])
        rlc = st[:, 1:2]
        fw.emit("dve", lambda e, rlc=rlc, acol=acol: e.reciprocal(rlc, acol),
                reads=[st], writes=[st])
        fw.emit("dve", lambda e, rlc=rlc: e.tensor_scalar(
            rlc, rlc, 127.0, None, ALU.mult), reads=[st], writes=[st])
        lsc = st[:, 2:3]
        fw.emit("dve", lambda e, lsc=lsc, acol=acol: e.tensor_scalar(
            lsc, acol, 1.0 / 127.0, None, ALU.mult), reads=[st], writes=[st])
        q = B["scrA"][:, 0:1024]
        fw.emit("dve", lambda e, q=q, rlc=rlc: e.tensor_scalar(
            q, q, rlc, None, ALU.mult), reads=[B["scrA"], st], writes=[B["scrA"]])
        fw.emit("dve", lambda e, q=q: e.tensor_scalar(q, q, 1.0, RC, ALU.mult, ALU.add),
                reads=[B["scrA"]], writes=[B["scrA"]])
        fw.emit("dve", lambda e, q=q: e.tensor_scalar(q, q, -RC, None, ALU.add),
                reads=[B["scrA"]], writes=[B["scrA"]])
        fw.emit("dve", lambda e, q=q: e.tensor_scalar(q, q, 127.0, -127.0,
                                                      ALU.min, ALU.max),
                reads=[B["scrA"]], writes=[B["scrA"]])
        for sj in range(8):
            fw.emit("pe", lambda e, q=q, sj=sj: e.transpose(
                psT, q[:, sj * 128:(sj + 1) * 128], B["idf"][:, :]),
                reads=[B["scrA"], B["idf"]], writes=[B["PS1"]])
            fw.emit("dve", lambda e, sj=sj: e.tensor_copy(
                B["fmt"][:, sj * 128:(sj + 1) * 128], psT),
                reads=[B["PS1"]], writes=[B["fmt"]])
        for kb in range(8):
            for nh in range(2):
                fw.emit("pe", lambda e, kb=kb, nh=nh: e.matmul(
                    B["PS2"][:, nh * 512:(nh + 1) * 512],
                    B["fmt"][:, kb * 128:(kb + 1) * 128],
                    qrB[:, kb * 1024 + nh * 512: kb * 1024 + (nh + 1) * 512],
                    start=(kb == 0), stop=(kb == 7)),
                    reads=[B["fmt"], B["KT"]], writes=[B["PS2"]])
        t1 = B["scrA"][:, 0:1024]
        fw.emit("dve", lambda e, t1=t1, lsc=lsc: e.tensor_scalar(
            t1, B["PS2"][:, 0:1024], lsc, None, ALU.mult),
            reads=[B["PS2"], st], writes=[B["scrA"]])
        fw.emit("dve", lambda e, t1=t1: e.tensor_tensor(t1, t1, RSB, ALU.mult),
                reads=[B["scrA"], B["wd"]], writes=[B["scrA"]])
        gt = B["scrB16"][:, 0:1024]
        fw.emit("act", lambda e, gt=gt, t1=t1: e.activation(gt, t1, AF.Silu),
                reads=[B["scrA"]], writes=[B["scrB16"]])
        dst = drams["gscr"][ti * 128:(ti + 1) * 128, :]
        fw.emit("sp", lambda e, dst=dst, gt=gt: e.dma_start(out=dst, in_=gt),
                reads=[B["scrB16"]], writes=[("gscr", ti)], dma=True)

    for ti in range(NT):
        s2 = B["scrB16"][:, 0:1024]
        src = drams["yscr"][ti * 128:(ti + 1) * 128, :]
        fw.emit("sp", lambda e, s2=s2, src=src: e.dma_start(out=s2, in_=src),
                reads=[("yscr", p_) for p_ in range(16)], writes=[B["scrB16"]],
                dma=True)
        gt = B["scrB16"][:, 1024:2048]
        src = drams["gscr"][ti * 128:(ti + 1) * 128, :]
        fw.emit("sp", lambda e, gt=gt, src=src: e.dma_start(out=gt, in_=src),
                reads=[("gscr", ti)], writes=[B["scrB16"]], dma=True)
        prod = B["scrA"][:, 0:1024]
        fw.emit("dve", lambda e, prod=prod, s2=s2, gt=gt: e.tensor_tensor(
            prod, s2, gt, ALU.mult),
            reads=[B["scrB16"]], writes=[B["scrA"]])
        xt, xk = ld_xtile(fw, B, xdram, ti)
        fw.emit("dve", lambda e, xt=xt, prod=prod: e.tensor_tensor(
            xt, xt, prod, ALU.add),
            reads=[B["scrA"], xk], writes=[xk])
        st_xtile(fw, B, xdram, ti, xt, [xk])



def build_A(layers=(0,), dbg=False, imp_out=False):
    nc = bass.Bass("TRN2", target_bir_lowering=False)
    ins = {}
    def inp(name, shape, dtype=F32):
        ins[name] = nc.dram_tensor(name, list(shape), dtype, kind="ExternalInput").ap()
    inp("X", (T, C))
    inp("IDF", (128, 128), F32)
    inp("IDB", (128, 128), BF16)
    if 0 in layers:
        inp("WQT", (C, C)); inp("WKT", (C, C)); inp("WVT", (C, C)); inp("WOT", (C, C))
        inp("BO", (1, C))
        inp("wqg", (1024, 4096), BF16); inp("wqu", (1024, 4096), BF16)
        inp("wqd", (4096, 1024), BF16)
        inp("s_in", (1, 1)); inp("s_in_down", (1, 1))
    if 1 in layers:
        for nm in ("MAAX", "MAAW", "MAAK", "MAAV", "MAAR", "MAAG", "TDEC", "UCOL",
                   "FF1G", "FF1B"):
            inp(nm, (128, 8))
        inp("TW1", (1024, 160)); inp("DW1", (1024, 64)); inp("DW2", (64, 1024))
        for s5 in range(5):
            inp("TW2_%d" % s5, (32, 1024))
        inp("WR", (C, C)); inp("WK", (C, C)); inp("WV", (C, C)); inp("WG", (C, C))
        inp("WO1", (C, C))
        inp("LNW", (1, C)); inp("LNB", (1, C))
        inp("FC1Q", (1024, 4096), I32); inp("FC1S", (8, 4096)); inp("FC1B", (1, 4096))
        inp("FC2Q", (4096, 1024), I32); inp("FC2S", (32, 1024)); inp("FC2B", (1, 1024))
        inp("INDIC", (128, 128), BF16)
        inp("UMASK", (64, 1024), BF16)
    if 2 in layers:
        inp("QW2", (C, C)); inp("KW2", (C, C)); inp("VW2", (C, C))
        inp("BQC", (128, 8)); inp("BKC", (128, 8)); inp("BVR", (1, C))
        inp("WPJI", (128, 256)); inp("LBW", (1, 16))
        inp("RHS", (C, C))
    XO = nc.dram_tensor("XO", [T, C], F32, kind="ExternalOutput").ap()
    IMP = nc.dram_tensor("IMP", [128, 16], F32, kind="ExternalOutput").ap() \
        if imp_out else None
    knd = "ExternalOutput" if dbg else "Internal"
    drams = {
        "x": nc.dram_tensor("xdram", [T, C], F32, kind="Internal").ap(),
        "gscr": nc.dram_tensor("gscr", [T, C], BF16, kind=knd).ap(),
        "g2scr": nc.dram_tensor("g2scr", [T, C], BF16, kind=knd).ap(),
        "yscr": nc.dram_tensor("yscr", [T, C], BF16, kind=knd).ap(),
        "vscr": nc.dram_tensor("vscr", [T, C], BF16, kind=knd).ap(),
    }
    if 2 in layers:
        drams["zqT"] = nc.dram_tensor("zqT", [16, T], F32, kind="Internal").ap()
    if dbg:
        drams["sdbg"] = nc.dram_tensor("sdbg", [T, C], F32, kind="ExternalOutput").ap()
        drams["mdbg"] = nc.dram_tensor("mdbg", [128, 16384], BF16, kind="ExternalOutput").ap()
        drams["tdbg"] = nc.dram_tensor("tdbg", [96, 2048], BF16, kind="ExternalOutput").ap()
        for nm_ in ("rdbg", "kdbg", "lddbg"):
            drams[nm_] = nc.dram_tensor(nm_, [128, 16384], BF16, kind="ExternalOutput").ap()
        drams["sdump"] = nc.dram_tensor("sdump", [128, 512], F32, kind="ExternalOutput").ap()
    with ExitStack() as ctx:
        E = ctx.enter_context
        fw = FW(nc, ctx)
        B = {}
        B["xstg"] = E(nc.sbuf_tensor("xstg", [128, 2 * C], F32))
        B["fm"] = E(nc.sbuf_tensor("fm", [128, 8 * T], BF16))
        B["ot"] = E(nc.sbuf_tensor("ot", [128, 8 * T], BF16))
        B["stats"] = E(nc.sbuf_tensor("stats", [128, 128], F32))
        B["cols"] = E(nc.sbuf_tensor("cols", [128, 8], F32))
        B["ccols"] = E(nc.sbuf_tensor("ccols", [128, 96], F32))
        B["s2"] = E(nc.sbuf_tensor("s2", [128, 2], F32))
        B["ones1"] = E(nc.sbuf_tensor("ones1", [1, 128], F32))
        B["ones1b"] = E(nc.sbuf_tensor("ones1b", [1, 128], BF16))
        B["ones64"] = E(nc.sbuf_tensor("ones64", [128, 64], BF16))
        B["idf"] = E(nc.sbuf_tensor("idf", [128, 128], F32))
        B["idb"] = E(nc.sbuf_tensor("idb", [128, 128], BF16))
        B["scrA"] = E(nc.sbuf_tensor("scrA", [128, 1536], F32))
        B["scrB16"] = E(nc.sbuf_tensor("scrB16", [128, 2048], BF16))
        B["qsb"] = {
            "absx": E(nc.sbuf_tensor("q_absx", [128, 1024], F32)),
            "bm": E(nc.sbuf_tensor("q_bm", [128, 480], F32)),
            "sc": E(nc.sbuf_tensor("q_sc", [128, 32], F32)),
            "dq": E(nc.sbuf_tensor("q_dq", [128, 32], F32)),
            "dqe": E(nc.sbuf_tensor("q_dqe", [128, 512], F32)),
            "rdqe": E(nc.sbuf_tensor("q_rdqe", [128, 512], F32)),
            "q8t": E(nc.sbuf_tensor("q_q8t", [128, 128], F32)),
            "q4t": E(nc.sbuf_tensor("q_q4t", [128, 3 * 512], F32)),
            "zz": E(nc.sbuf_tensor("q_zz", [128, 512], F32)),
        }
        B["wgu"] = E(nc.sbuf_tensor("wgu", [128, 8 * 512], BF16))
        B["wgu2"] = E(nc.sbuf_tensor("wgu2", [128, 8 * 512], BF16))
        B["wd"] = E(nc.sbuf_tensor("wd", [128, 2 * 1024], BF16))
        B["fmt"] = E(nc.sbuf_tensor("fmt", [128, 1024], BF16))
        B["wq_p"] = E(nc.sbuf_tensor("wq_p", [128, 8 * 128], BF16))
        B["wk_p"] = E(nc.sbuf_tensor("wk_p", [128, 8 * 128], BF16))
        B["wv_p"] = E(nc.sbuf_tensor("wv_p", [128, 8 * 128], BF16))
        B["wo_c"] = E(nc.sbuf_tensor("wo_c", [128, 1024], BF16))
        B["qt2"] = E(nc.sbuf_tensor("qt2", [128, T], BF16))
        B["kt2"] = E(nc.sbuf_tensor("kt2", [128, T], BF16))
        B["v2a"] = E(nc.sbuf_tensor("v2a", [128, NT * 130], BF16))
        B["ptt"] = E(nc.sbuf_tensor("ptt", [128, T], BF16))
        B["bo_b"] = E(nc.sbuf_tensor("bo_b", [128, C], BF16))
        if any(l in layers for l in (1, 2)):
            B["KT"] = E(nc.sbuf_tensor("KT", [128, 8 * T], BF16))
            B["vstg"] = E(nc.sbuf_tensor("vstg", [128, 1024], BF16))
            B["w1sb"] = E(nc.sbuf_tensor("w1sb", [128, 1280], BF16))
            B["indic"] = E(nc.sbuf_tensor("indic", [128, 128], BF16))
            B["umask"] = E(nc.sbuf_tensor("umask", [64, 1024], BF16))
            B["bonT"] = E(nc.sbuf_tensor("bonT", [128, 256], BF16))
            B["Sbig"] = E(nc.sbuf_tensor("Sbig", [128, 512], F32))
            B["Sb16"] = E(nc.sbuf_tensor("Sb16", [128, 512], BF16))
        B["RT"] = B["fm"]
        B["bonF"] = B["xstg"]
        B["LD"] = B["ot"]
        B["rb16"] = B["scrB16"]
        B["lrow"] = B["ptt"]
        B["dw1sb"] = B["wq_p"]
        B["dw2sb"] = B["wk_p"]
        B["mixbuf"] = B["wgu"]
        B["mixTa"] = B["qt2"]
        B["mixTb"] = B["kt2"]
        B["twT"] = B["ptt"]
        PS1 = E(nc.psum_tensor("PS1", [128, 2048], F32))
        PS2 = E(nc.psum_tensor("PS2", [128, 2048], F32))
        B["PS1"], B["PS2"] = PS1, PS2
        B["psA"] = PS1
        B["psO"] = PS2
        B["psG"] = PS1
        B["psU"] = PS1
        B["psD"] = PS2
        B["psV"] = PS1
        B["psT"] = PS1
        block = E(nc.Block())

        fw.emit("sp", lambda e: e.dma_start(out=drams["x"], in_=ins["X"]),
                writes=[("xdram", t) for t in range(NT)], dma=True)
        fw.emit("sp", lambda e: e.dma_start(out=B["idf"][:, :], in_=ins["IDF"]),
                writes=[B["idf"]], dma=True)
        fw.emit("sp", lambda e: e.dma_start(out=B["idb"][:, :], in_=ins["IDB"]),
                writes=[B["idb"]], dma=True)
        fw.emit("dve", lambda e: e.memset(B["ones1"][:, :], 1.0), writes=[B["ones1"]])
        fw.emit("dve", lambda e: e.memset(B["ones1b"][:, :], 1.0), writes=[B["ones1b"]])
        if 1 in layers:
            fw.emit("sp", lambda e: e.dma_start(out=B["indic"][:, :], in_=ins["INDIC"]),
                    writes=[B["indic"]], dma=True)
            fw.emit("sp", lambda e: e.dma_start(out=B["umask"][:, :], in_=ins["UMASK"]),
                    writes=[B["umask"]], dma=True)
        if 0 in layers:
            emit_l0(fw, B, ins, drams)
        if 1 in layers:
            emit_l1(fw, B, ins, drams)
        if 2 in layers:
            emit_l2(fw, B, ins, drams)
        if imp_out:
            emit_imp(fw, B, drams, IMP)
        for ti in range(NT):
            xt, xk = ld_xtile(fw, B, drams["x"], ti)
            dst = XO[ti * 128:(ti + 1) * 128, :]
            fw.emit("sp", lambda e, dst=dst, xt=xt: e.dma_start(out=dst, in_=xt),
                    reads=[xk], writes=[("XO", ti)], dma=True)
        fw.drain_outputs()
        fw.finalize(block)
    return nc


def _cols128x8(v):
    return np.ascontiguousarray(np.asarray(v, np.float32).reshape(8, 128).T)


def run_A(I, wq, layers=(0,), dbg=False, x0=None, imp_out=False):
    import ml_dtypes
    nc = build_A(layers, dbg=dbg, imp_out=imp_out)
    gq, uq, dq = wq
    com = {
        "IDF": np.eye(128, dtype=np.float32),
        "IDB": np.eye(128, dtype=np.float32).astype(ml_dtypes.bfloat16),
    }
    if 0 in layers:
        com.update({
            "WQT": np.ascontiguousarray(I["L0_attn_Wq"].T),
            "WKT": np.ascontiguousarray(I["L0_attn_Wk"].T),
            "WVT": np.ascontiguousarray(I["L0_attn_Wv"].T),
            "WOT": np.ascontiguousarray(I["L0_attn_Wo"].T),
            "BO": I["L0_attn_bo"].reshape(1, C),
            "wqg": np.ascontiguousarray(gq.T),
            "wqu": np.ascontiguousarray(uq.T),
            "wqd": np.ascontiguousarray(dq.T),
            "s_in": I["L0_ffn_s_in"].reshape(1, 1),
            "s_in_down": I["L0_ffn_s_in_down"].reshape(1, 1),
        })
    if 1 in layers:
        indic = np.zeros((128, 128), np.float32)
        for k in range(8):
            indic[0:64, k * 16 + 2 * k] = 1.0
            indic[64:128, k * 16 + 2 * k + 1] = 1.0
        umask1 = (np.arange(64)[:, None] < np.arange(64)[None, :]).astype(np.float32)
        umask = np.tile(umask1, (1, 16))
        com.update({
            "MAAX": _cols128x8(I["L1_attn_time_maa_x"]),
            "MAAW": _cols128x8(I["L1_attn_time_maa_w"]),
            "MAAK": _cols128x8(I["L1_attn_time_maa_k"]),
            "MAAV": _cols128x8(I["L1_attn_time_maa_v"]),
            "MAAR": _cols128x8(I["L1_attn_time_maa_r"]),
            "MAAG": _cols128x8(I["L1_attn_time_maa_g"]),
            "TDEC": _cols128x8(I["L1_attn_time_decay"]),
            "UCOL": _cols128x8(I["L1_attn_time_faaaa"].reshape(-1)),
            "FF1G": _cols128x8(I["L1_ffn_ln_gamma"]),
            "FF1B": _cols128x8(I["L1_ffn_ln_beta"]),
            "TW1": np.ascontiguousarray(I["L1_attn_time_maa_w1"]),
            "DW1": np.ascontiguousarray(I["L1_attn_time_decay_w1"]),
            "DW2": np.ascontiguousarray(I["L1_attn_time_decay_w2"]),
            "WR": np.ascontiguousarray(I["L1_attn_Wr"]),
            "WK": np.ascontiguousarray(I["L1_attn_Wk"]),
            "WV": np.ascontiguousarray(I["L1_attn_Wv"]),
            "WG": np.ascontiguousarray(I["L1_attn_Wg"]),
            "WO1": np.ascontiguousarray(I["L1_attn_Wo"]),
            "LNW": I["L1_attn_ln_w"].reshape(1, C),
            "LNB": I["L1_attn_ln_b"].reshape(1, C),
            "FC1Q": np.ascontiguousarray(I["L1_ffn_fc1_q"]),
            "FC1S": np.ascontiguousarray(I["L1_ffn_fc1_scales"]),
            "FC1B": I["L1_ffn_fc1_bias"].reshape(1, 4096),
            "FC2Q": np.ascontiguousarray(I["L1_ffn_fc2_q"]),
            "FC2S": np.ascontiguousarray(I["L1_ffn_fc2_scales"]),
            "FC2B": I["L1_ffn_fc2_bias"].reshape(1, 1024),
            "INDIC": indic.astype(ml_dtypes.bfloat16),
            "UMASK": umask.astype(ml_dtypes.bfloat16),
        })
        for s5 in range(5):
            com["TW2_%d" % s5] = np.ascontiguousarray(I["L1_attn_time_maa_w2"][s5])
    if 2 in layers:
        Wproj = np.asarray(I["L2_attn_Wproj"], np.float32)
        WPJ = np.zeros((128, 256), np.float32)
        for j in range(16):
            for d in range(64):
                WPJ[d, j * 16:(j + 1) * 16] = Wproj[:, 64 * j + d]
                WPJ[64 + d, j * 16:(j + 1) * 16] = Wproj[:, 64 * j + d]
        com.update({
            "QW2": np.ascontiguousarray(I["L2_attn_Wq"].T),
            "KW2": np.ascontiguousarray(I["L2_attn_Wk"].T),
            "VW2": np.ascontiguousarray(I["L2_attn_Wv"].T),
            "BQC": _cols128x8(I["L2_attn_bq"]),
            "BKC": _cols128x8(I["L2_attn_bk"]),
            "BVR": I["L2_attn_bv"].reshape(1, C).astype(np.float32),
            "WPJI": WPJ,
            "LBW": I["L2_attn_log_bandwidth"].reshape(1, 16).astype(np.float32),
            "RHS": np.ascontiguousarray(I["L2_ffn_rhs"]),
        })
    if x0 is None:
        x0 = I["L0_attn_x"]
    in_maps = [dict(com, X=np.ascontiguousarray(x0[c])) for c in range(NC)]
    res = run_bass_kernel_spmd(nc, in_maps, core_ids=list(range(NC))).results
    xo = np.stack([res[c]["XO"] for c in range(NC)], 0)
    run_A.dbg = {k: np.stack([res[c][k] for c in range(NC)], 0)
                 for k in res[0] if k != "XO"}
    if imp_out:
        imp = np.stack([np.asarray(res[c]["IMP"]).T.reshape(T)
                        for c in range(NC)], 0)
        return xo, imp
    return xo



R = 2048
EXPERTS = [("e0", 4096, 1024), ("e1", 2048, 2048), ("e2", 3072, 1024),
           ("e0", 4096, 1024), ("e1", 2048, 2048), ("e2", 3072, 1024),
           ("e0", 4096, 1024), ("e1", 2048, 2048)]
PCT_K = 14744

B_STOP = None


def build_B():
    nc = bass.Bass("TRN2", target_bir_lowering=False)
    ins = {}
    def inp(name, shape, dtype=F32):
        ins[name] = nc.dram_tensor(name, list(shape), dtype, kind="ExternalInput").ap()
    inp("XB", (R, C))
    inp("IMPF", (128, 128))
    inp("IMPSC", (128, 16))
    inp("IDF", (128, 128))
    inp("UW2", (C, 2048)); inp("BUC", (128, 16))
    inp("VW3", (C, 2048)); inp("BVC", (128, 16))
    inp("WO3T", (2048, C)); inp("BO3R", (1, C))
    inp("CCOEF", (128, 64))
    inp("GW1", (C, C)); inp("GB1R", (1, C))
    inp("LNSR", (1, C)); inp("LNOR", (1, C))
    inp("GW2", (C, 512)); inp("GB2R", (1, 512))
    inp("GW3C", (128, 32)); inp("GB3R", (1, 8))
    for pi, (kind, H1, H2) in enumerate(EXPERTS):
        inp("EW1_%d" % pi, (C, H1)); inp("EB1_%d" % pi, (1, H1))
        inp("EW2_%d" % pi, (H1, H2)); inp("EB2_%d" % pi, (1, H2))
        if kind == "e1":
            inp("EW3_%d" % pi, (2048, C)); inp("EB3_%d" % pi, (1, C))
    S3O = nc.dram_tensor("S3O", [R, C], F32, kind="ExternalOutput").ap()
    XNO = nc.dram_tensor("XNO", [R, C], BF16, kind="ExternalOutput").ap()
    LGO = nc.dram_tensor("LGO", [R, 8], F32, kind="ExternalOutput").ap()
    PAFFO = nc.dram_tensor("PAFFO", [1, 8 * C], F32, kind="ExternalOutput").ap()
    EXO = [nc.dram_tensor("EXO_%d" % pi, [R, C], BF16, kind="ExternalOutput").ap()
           for pi in range(8)]
    hscr = nc.dram_tensor("hscr", [R, 4096], BF16, kind="Internal").ap()
    with ExitStack() as ctx:
        E = ctx.enter_context
        fw = FW(nc, ctx)
        B = {}
        B["xstg"] = E(nc.sbuf_tensor("xstg", [128, 2048], F32))
        B["FMA"] = E(nc.sbuf_tensor("FMA", [128, 16384], BF16))
        B["FMV"] = E(nc.sbuf_tensor("FMV", [128, 32768], BF16))
        B["wres"] = E(nc.sbuf_tensor("wres", [128, 8192], BF16))
        B["wstr"] = E(nc.sbuf_tensor("wstr", [128, 4096], BF16))
        B["fmt"] = E(nc.sbuf_tensor("fmtB", [128, 4096], BF16))
        B["scrA"] = E(nc.sbuf_tensor("scrA", [128, 2048], F32))
        B["scrB16"] = E(nc.sbuf_tensor("scrB16", [128, 2048], BF16))
        B["biasb"] = E(nc.sbuf_tensor("biasb", [128, 2048], F32))
        B["stats"] = E(nc.sbuf_tensor("statsB", [128, 128], F32))
        B["impf"] = E(nc.sbuf_tensor("impf", [128, 128], F32))
        B["impc"] = E(nc.sbuf_tensor("impc", [128, 16], F32))
        B["ccoef"] = E(nc.sbuf_tensor("ccoef", [128, 64], F32))
        B["gw3c"] = E(nc.sbuf_tensor("gw3c", [128, 32], BF16))
        B["lgb"] = E(nc.sbuf_tensor("lgb", [128, 16], F32))
        B["ones1"] = E(nc.sbuf_tensor("ones1", [1, 128], F32))
        B["onesc"] = E(nc.sbuf_tensor("onesc", [128, 1], F32))
        B["onescb"] = E(nc.sbuf_tensor("onescb", [128, 1], BF16))
        B["idf"] = E(nc.sbuf_tensor("idf", [128, 128], F32))
        PS1 = E(nc.psum_tensor("PS1", [128, 2048], F32))
        PS2 = E(nc.psum_tensor("PS2", [128, 2048], F32))
        B["PS1"], B["PS2"] = PS1, PS2
        block = E(nc.Block())
        st = B["stats"]
        psT = PS1[:, 1024:1152]

        fw.emit("sp", lambda e: e.dma_start(out=B["idf"][:, :], in_=ins["IDF"]),
                writes=[B["idf"]], dma=True)
        fw.emit("sp", lambda e: e.dma_start(out=B["impf"][:, :], in_=ins["IMPF"]),
                writes=[B["impf"]], dma=True)
        fw.emit("sp", lambda e: e.dma_start(out=B["impc"][:, :], in_=ins["IMPSC"]),
                writes=[B["impc"]], dma=True)
        fw.emit("sp", lambda e: e.dma_start(out=B["ccoef"][:, :], in_=ins["CCOEF"]),
                writes=[B["ccoef"]], dma=True)
        fw.emit("pool", lambda e: e.dma_start(out=B["gw3c"][:, :], in_=ins["GW3C"]),
                writes=[B["gw3c"]], dma=True)
        fw.emit("dve", lambda e: e.memset(B["ones1"][:, :], 1.0), writes=[B["ones1"]])
        fw.emit("dve", lambda e: e.memset(B["onesc"][:, :], 1.0), writes=[B["onesc"]])
        fw.emit("dve", lambda e: e.memset(B["onescb"][:, :], 1.0), writes=[B["onescb"]])

        def scal_bcast(src11, dstcol):
            fw.emit("pe", lambda e, src11=src11: e.matmul(
                PS1[:, 1136:1137], B["ones1"][:, :], src11, start=True, stop=True),
                reads=[st, B["ones1"]], writes=[B["PS1"]])
            fw.emit("dve", lambda e, dstcol=dstcol: e.tensor_copy(
                dstcol, PS1[:, 1136:1137]), reads=[B["PS1"]], writes=[st])

        def psum_scalar(col, dst11):
            fw.emit("pe", lambda e, col=col, dst11=dst11: e.matmul(
                PS1[0:1, 1136:1137], col, B["onesc"][:, :], start=True, stop=True),
                reads=[st, B["onesc"]], writes=[B["PS1"]])
            fw.emit("dve", lambda e, dst11=dst11: e.tensor_copy(
                dst11, PS1[0:1, 1136:1137]), reads=[B["PS1"]], writes=[st])

        LO, HI, MID, CNT, VK, VK1, THR, TMP1, TMP2 = (
            st[0:1, 64 + i:65 + i] for i in range(9))
        midc = st[:, 80:81]
        tcol = st[:, 81:82]
        thrc = st[:, 82:83]

        rc = st[:, 83:84]
        fw.emit("dve", lambda e: e.tensor_reduce(rc, B["impf"][:, :], AX.X, ALU.max),
                reads=[B["impf"]], writes=[st])
        fw.emit("pe", lambda e: e.transpose(PS1[0:1, 1024:1152], rc, B["idf"][:, :]),
                reads=[st, B["idf"]], writes=[B["PS1"]])
        fw.emit("dve", lambda e: e.tensor_reduce(
            HI, PS1[0:1, 1024:1152], AX.X, ALU.max), reads=[B["PS1"]], writes=[st])
        fw.emit("dve", lambda e: e.tensor_scalar(HI, HI, 1.0, None, ALU.add),
                reads=[st], writes=[st])
        fw.emit("dve", lambda e: e.memset(LO, 0.0), writes=[st])
        u8 = lambda ap: ap.bitcast(U8)[0:1, 0:4]
        cntmask = B["scrA"][:, 0:128]
        ccol = st[:, 84:85]
        for it in range(40):
            fw.emit("dve", lambda e: e.tensor_tensor(MID, LO, HI, ALU.add),
                    reads=[st], writes=[st])
            fw.emit("dve", lambda e: e.tensor_scalar(MID, MID, 0.5, None, ALU.mult),
                    reads=[st], writes=[st])
            scal_bcast(MID, midc)
            fw.emit("dve", lambda e: e.tensor_scalar(
                cntmask, B["impf"][:, :], midc, None, ALU.is_le),
                reads=[B["impf"], st], writes=[B["scrA"]])
            fw.emit("dve", lambda e: e.tensor_reduce(ccol, cntmask, AX.X, ALU.add),
                    reads=[B["scrA"]], writes=[st])
            psum_scalar(ccol, CNT)
            cm = st.bitcast(U8)[0:1, (64 + 7) * 4:(64 + 7) * 4 + 1]
            fw.emit("dve", lambda e, cm=cm: e.tensor_scalar(
                cm, CNT, float(PCT_K + 1), None, ALU.is_ge),
                reads=[st], writes=[st])
            fw.emit("dve", lambda e, cm=cm: e.select(HI, cm, MID, HI),
                    reads=[st], writes=[st])
            fw.emit("dve", lambda e, cm=cm: e.select(LO, cm, LO, MID),
                    reads=[st], writes=[st])
        scal_bcast(HI, midc)
        fw.emit("dve", lambda e: e.tensor_scalar(
            cntmask, B["impf"][:, :], midc, None, ALU.is_le),
            reads=[B["impf"], st], writes=[B["scrA"]])
        msel = B["scrA"][:, 128:256]
        fw.emit("dve", lambda e: e.tensor_tensor(
            msel, cntmask, B["impf"][:, :], ALU.mult),
            reads=[B["scrA"], B["impf"]], writes=[B["scrA"]])
        fw.emit("dve", lambda e: e.tensor_reduce(rc, msel, AX.X, ALU.max),
                reads=[B["scrA"]], writes=[st])
        fw.emit("pe", lambda e: e.transpose(PS1[0:1, 1024:1152], rc, B["idf"][:, :]),
                reads=[st, B["idf"]], writes=[B["PS1"]])
        fw.emit("dve", lambda e: e.tensor_reduce(
            VK, PS1[0:1, 1024:1152], AX.X, ALU.max), reads=[B["PS1"]], writes=[st])
        scal_bcast(VK, midc)
        fw.emit("dve", lambda e: e.tensor_scalar(
            cntmask, B["impf"][:, :], midc, None, ALU.is_le),
            reads=[B["impf"], st], writes=[B["scrA"]])
        fw.emit("dve", lambda e: e.tensor_reduce(ccol, cntmask, AX.X, ALU.add),
                reads=[B["scrA"]], writes=[st])
        psum_scalar(ccol, CNT)
        BIG = 1e30
        fw.emit("dve", lambda e: e.tensor_scalar(
            cntmask, B["impf"][:, :], midc, None, ALU.is_le),
            reads=[B["impf"], st], writes=[B["scrA"]])
        fw.emit("dve", lambda e: e.tensor_scalar(
            cntmask, cntmask, BIG, None, ALU.mult), reads=[B["scrA"]], writes=[B["scrA"]])
        fw.emit("dve", lambda e: e.tensor_tensor(
            msel, cntmask, B["impf"][:, :], ALU.add),
            reads=[B["scrA"], B["impf"]], writes=[B["scrA"]])
        fw.emit("dve", lambda e: e.tensor_reduce(rc, msel, AX.X, ALU.min),
                reads=[B["scrA"]], writes=[st])
        fw.emit("pe", lambda e: e.transpose(PS1[0:1, 1024:1152], rc, B["idf"][:, :]),
                reads=[st, B["idf"]], writes=[B["PS1"]])
        fw.emit("dve", lambda e: e.tensor_reduce(
            TMP2, PS1[0:1, 1024:1152], AX.X, ALU.min), reads=[B["PS1"]], writes=[st])
        cm2 = st.bitcast(U8)[0:1, (64 + 7) * 4:(64 + 7) * 4 + 1]
        fw.emit("dve", lambda e, cm2=cm2: e.tensor_scalar(
            cm2, CNT, float(PCT_K + 2), None, ALU.is_ge), reads=[st], writes=[st])
        fw.emit("dve", lambda e, cm2=cm2: e.select(VK1, cm2, VK, TMP2),
                reads=[st], writes=[st])
        fw.emit("dve", lambda e: e.tensor_scalar(THR, VK, 0.3, None, ALU.mult),
                reads=[st], writes=[st])
        fw.emit("dve", lambda e: e.tensor_scalar(TMP2, VK1, 0.7, None, ALU.mult),
                reads=[st], writes=[st])
        fw.emit("dve", lambda e: e.tensor_tensor(THR, THR, TMP2, ALU.add),
                reads=[st], writes=[st])
        scal_bcast(THR, thrc)
        fw.emit("dve", lambda e: e.tensor_reduce(rc, B["impf"][:, :], AX.X, ALU.add),
                reads=[B["impf"]], writes=[st])
        psum_scalar(rc, TMP1)
        fw.emit("act", lambda e: e.activation(
            msel[0:128, 0:128], B["impf"][:, :], AF.Square, accum_out=rc),
            reads=[B["impf"]], writes=[B["scrA"], st])
        psum_scalar(rc, TMP2)
        N16 = float(128 * 128)
        fw.emit("dve", lambda e: e.tensor_scalar(TMP1, TMP1, 1.0 / N16, None, ALU.mult),
                reads=[st], writes=[st])
        fw.emit("dve", lambda e: e.tensor_scalar(TMP2, TMP2, 1.0 / N16, None, ALU.mult),
                reads=[st], writes=[st])
        fw.emit("dve", lambda e: e.tensor_tensor(TMP1, TMP1, TMP1, ALU.mult),
                reads=[st], writes=[st])
        fw.emit("dve", lambda e: e.tensor_tensor(TMP2, TMP2, TMP1, ALU.subtract),
                reads=[st], writes=[st])
        fw.emit("act", lambda e: e.activation(TMP2, TMP2, AF.Sqrt),
                reads=[st], writes=[st])
        fw.emit("dve", lambda e: e.tensor_scalar(TMP2, TMP2, 0.1, 1.0,
                                                 ALU.mult, ALU.add),
                reads=[st], writes=[st])
        scal_bcast(TMP2, tcol)
        if B_STOP == "stats":
            fw.emit("sp", lambda e: e.dma_start(out=PAFFO[0:1, 0:128],
                    in_=st[0:1, 0:128]), reads=[st], dma=True)
            fw.drain_outputs()
            fw.finalize(block)
            return nc

        negmuc = st[:, 96:97]
        rstdc = st[:, 97:98]

        def emit_tile_ln(rt, dst):
            xbt = B["xstg"][:, (rt % 2) * 1024:(rt % 2) * 1024 + 1024]
            src = ins["XB"][rt * 128:(rt + 1) * 128, :]
            fw.emit("sp", lambda e, xbt=xbt, src=src: e.dma_start(out=xbt, in_=src),
                    writes=[B["xstg"]], dma=True)
            fw.emit("dve", lambda e, xbt=xbt: e.tensor_reduce(
                st[:, 98:99], xbt, AX.X, ALU.add), reads=[B["xstg"]], writes=[st])
            fw.emit("act", lambda e, xbt=xbt: e.activation(
                B["scrA"][:, 0:1024], xbt, AF.Square, accum_out=st[:, 99:100]),
                reads=[B["xstg"]], writes=[B["scrA"], st])
            fw.emit("dve", lambda e: e.tensor_scalar(
                st[:, 100:101], st[:, 98:99], 1.0 / C, None, ALU.mult),
                reads=[st], writes=[st])
            fw.emit("dve", lambda e: e.tensor_scalar(
                st[:, 101:102], st[:, 99:100], 1.0 / C, None, ALU.mult),
                reads=[st], writes=[st])
            fw.emit("dve", lambda e: e.tensor_tensor(
                st[:, 102:103], st[:, 100:101], st[:, 100:101], ALU.mult),
                reads=[st], writes=[st])
            fw.emit("dve", lambda e: e.tensor_tensor(
                st[:, 102:103], st[:, 101:102], st[:, 102:103], ALU.subtract),
                reads=[st], writes=[st])
            fw.emit("dve", lambda e: e.tensor_scalar(
                st[:, 102:103], st[:, 102:103], 1e-5, None, ALU.add),
                reads=[st], writes=[st])
            fw.emit("dve", lambda e: e.reciprocal(st[:, 103:104], st[:, 102:103]),
                    reads=[st], writes=[st])
            fw.emit("act", lambda e: e.activation(
                rstdc, st[:, 103:104], AF.Sqrt), reads=[st], writes=[st])
            fw.emit("dve", lambda e: e.tensor_scalar(
                negmuc, st[:, 100:101], -1.0, None, ALU.mult),
                reads=[st], writes=[st])
            fw.emit("dve", lambda e, dst=dst, xbt=xbt: e.tensor_scalar(
                dst, xbt, negmuc, rstdc, ALU.add, ALU.mult),
                reads=[B["xstg"], st], writes=[B["scrA"]])

        FMA = B["FMA"]
        for rt in range(NT):
            xn = B["scrA"][:, 0:1024]
            emit_tile_ln(rt, xn)
            rmsc = st[:, 112:113]
            fw.emit("dve", lambda e, rt=rt, rmsc=rmsc: e.tensor_scalar(
                rmsc, B["impc"][:, rt:rt + 1], 1.0 / 32.0, 1e-8, ALU.mult, ALU.add),
                reads=[B["impc"]], writes=[st])
            fw.emit("dve", lambda e, rmsc=rmsc: e.reciprocal(st[:, 113:114], rmsc),
                    reads=[st], writes=[st])
            fw.emit("dve", lambda e, xn=xn: e.tensor_scalar(
                xn, xn, st[:, 113:114], None, ALU.mult),
                reads=[B["scrA"], st], writes=[B["scrA"]])
            for k in range(8):
                fw.emit("pe", lambda e, xn=xn, k=k: e.transpose(
                    psT, xn[:, k * 128:(k + 1) * 128], B["idf"][:, :]),
                    reads=[B["scrA"], B["idf"]], writes=[B["PS1"]])
                fw.emit("dve", lambda e, k=k, rt=rt: e.tensor_copy(
                    FMA[:, k * R + rt * 128:k * R + (rt + 1) * 128], psT),
                    reads=[B["PS1"]], writes=[B["FMA"]])

        FMV = B["FMV"]
        def uvbuild(wname, bname, consume_u):
            for oc in range(16):
                wc = B["wstr"][:, 0:1024]
                src = ins[wname].rearrange("(k p) m -> p k m", p=128)[
                    :, :, oc * 128:(oc + 1) * 128]
                fw.emit("pool", lambda e, wc=wc, src=src: e.dma_start(
                    out=wc.rearrange("p (k m) -> p k m", k=8), in_=src),
                    writes=[B["wstr"]], dma=True)
                for rb4 in range(4):
                    for k in range(8):
                        fw.emit("pe", lambda e, k=k, wc=wc, rb4=rb4: e.matmul(
                            PS2[:, 0:512], wc[:, k * 128:(k + 1) * 128],
                            FMA[:, k * R + rb4 * 512: k * R + (rb4 + 1) * 512],
                            start=(k == 0), stop=(k == 7)),
                            reads=[B["wstr"], B["FMA"]], writes=[B["PS2"]])
                    if not consume_u:
                        fw.emit("act", lambda e, oc=oc, rb4=rb4: e.activation(
                            FMV[:, oc * R + rb4 * 512: oc * R + (rb4 + 1) * 512],
                            PS2[:, 0:512], AF.Silu,
                            bias=B["impc"][:, 0:1] if False else
                            B["ccoef"][:, 0:1] if False else
                            B["biasb"][:, oc:oc + 1]),
                            reads=[B["PS2"], B["biasb"]], writes=[B["FMV"]])
                    else:
                        uo = B["scrA"][:, 0:512]
                        fw.emit("act", lambda e, uo=uo, oc=oc, rb4=rb4: e.activation(
                            uo, PS2[:, 0:512], AF.Silu,
                            bias=B["biasb"][:, oc:oc + 1]),
                            reads=[B["PS2"], B["biasb"]], writes=[B["scrA"]])
                        fw.emit("dve", lambda e, uo=uo, oc=oc, rb4=rb4: e.tensor_tensor(
                            FMV[:, oc * R + rb4 * 512: oc * R + (rb4 + 1) * 512],
                            FMV[:, oc * R + rb4 * 512: oc * R + (rb4 + 1) * 512],
                            uo, ALU.mult),
                            reads=[B["scrA"], B["FMV"]], writes=[B["FMV"]])
        fw.emit("sp", lambda e: e.dma_start(out=B["biasb"][:, 0:16], in_=ins["BVC"]),
                writes=[B["biasb"]], dma=True)
        uvbuild("VW3", "BVC", consume_u=False)
        for k in range(16):
            h = k // 2
            for n in range(7, -1, -1):
                acc = B["scrA"][:, 0:256]
                fw.emit("dve", lambda e, k=k, n=n, h=h, acc=acc: e.tensor_scalar(
                    acc, FMV[:, k * R + n * 256:k * R + (n + 1) * 256],
                    B["ccoef"][:, h * 8:h * 8 + 1], None, ALU.mult),
                    reads=[B["FMV"], B["ccoef"]], writes=[B["scrA"]])
                for j in range(n - 1, -1, -1):
                    fw.emit("dve", lambda e, k=k, n=n, j=j, h=h, acc=acc:
                            e.scalar_tensor_tensor(
                        acc, FMV[:, k * R + j * 256:k * R + (j + 1) * 256],
                        B["ccoef"][:, h * 8 + (n - j):h * 8 + (n - j) + 1],
                        acc, ALU.mult, ALU.add),
                        reads=[B["FMV"], B["ccoef"], B["scrA"]], writes=[B["scrA"]])
                fw.emit("dve", lambda e, k=k, n=n, acc=acc: e.tensor_copy(
                    FMV[:, k * R + n * 256:k * R + (n + 1) * 256], acc),
                    reads=[B["scrA"]], writes=[B["FMV"]])
        fw.emit("sp", lambda e: e.dma_start(out=B["biasb"][:, 0:16], in_=ins["BUC"]),
                writes=[B["biasb"]], dma=True)
        uvbuild("UW2", "BUC", consume_u=True)

        for cc in range(16):
            src = ins["WO3T"][cc * 128:(cc + 1) * 128, :]
            fw.emit("pool", lambda e, cc=cc, src=src: e.dma_start(
                out=FMA[:, cc * 1024:(cc + 1) * 1024], in_=src),
                writes=[B["FMA"]], dma=True)
        bo3b = B["biasb"][:, 1024:2048]
        fw.emit("sp", lambda e: e.dma_start(
            out=bo3b, in_=ins["BO3R"][0:1, :].broadcast_to([128, 1024])),
            writes=[B["biasb"]], dma=True)
        for rt in range(NT):
            for cc in range(16):
                for nh in range(2):
                    fw.emit("pe", lambda e, rt=rt, cc=cc, nh=nh: e.matmul(
                        PS2[:, nh * 512:(nh + 1) * 512],
                        FMV[:, cc * R + rt * 128: cc * R + (rt + 1) * 128],
                        FMA[:, cc * 1024 + nh * 512: cc * 1024 + (nh + 1) * 512],
                        start=(cc == 0), stop=(cc == 15)),
                        reads=[B["FMV"], B["FMA"]], writes=[B["PS2"]])
            xn = B["scrA"][:, 0:1024]
            emit_tile_ln(rt, xn)
            s3t = B["scrA"][:, 1024:2048]
            fw.emit("dve", lambda e, s3t=s3t: e.tensor_tensor(
                s3t, PS2[:, 0:1024], bo3b, ALU.add),
                reads=[B["PS2"], B["biasb"]], writes=[B["scrA"]])
            fw.emit("dve", lambda e, s3t=s3t, xn=xn: e.tensor_tensor(
                s3t, s3t, xn, ALU.add),
                reads=[B["scrA"]], writes=[B["scrA"]])
            dst = S3O[rt * 128:(rt + 1) * 128, :]
            fw.emit("sp", lambda e, dst=dst, s3t=s3t: e.dma_start(out=dst, in_=s3t),
                    reads=[B["scrA"]], writes=[("S3O", rt)], dma=True)
        if B_STOP == "attn":
            fw.drain_outputs()
            fw.finalize(block)
            return nc

        for rt in range(NT):
            xn = B["scrA"][:, 0:1024]
            emit_tile_ln(rt, xn)
            mcol = st[:, 103:104]
            fw.emit("dve", lambda e, rt=rt, mcol=mcol: e.tensor_scalar(
                mcol, B["impc"][:, rt:rt + 1], thrc, None, ALU.is_gt),
                reads=[B["impc"], st], writes=[st])
            srow = st[:, 104:105]
            fw.emit("dve", lambda e, xn=xn, srow=srow: e.tensor_reduce(
                srow, xn, AX.X, ALU.add, apply_absolute_value=True),
                reads=[B["scrA"]], writes=[st])
            fw.emit("dve", lambda e, srow=srow, mcol=mcol: e.tensor_tensor(
                srow, srow, mcol, ALU.mult), reads=[st], writes=[st])
            fw.emit("dve", lambda e, srow=srow: e.tensor_scalar(
                srow, srow, 1e-8, None, ALU.add), reads=[st], writes=[st])
            fw.emit("dve", lambda e, srow=srow: e.tensor_tensor(
                srow, srow, tcol, ALU.mult), reads=[st], writes=[st])
            rfac = st[:, 105:106]
            fw.emit("dve", lambda e, rfac=rfac, srow=srow: e.reciprocal(rfac, srow),
                    reads=[st], writes=[st])
            fw.emit("dve", lambda e, rfac=rfac, mcol=mcol: e.tensor_tensor(
                rfac, rfac, mcol, ALU.mult), reads=[st], writes=[st])
            xq32 = B["scrA"][:, 1024:2048]
            fw.emit("dve", lambda e, xq32=xq32, xn=xn, rfac=rfac: e.tensor_scalar(
                xq32, xn, rfac, None, ALU.mult),
                reads=[B["scrA"], st], writes=[B["scrA"]])
            xq = B["scrB16"][:, 0:1024]
            fw.emit("dve", lambda e, xq=xq, xq32=xq32: e.tensor_copy(xq, xq32),
                    reads=[B["scrA"]], writes=[B["scrB16"]])
            dst = XNO[rt * 128:(rt + 1) * 128, :]
            fw.emit("sp", lambda e, dst=dst, xq=xq: e.dma_start(out=dst, in_=xq),
                    reads=[B["scrB16"]], writes=[("XNO", rt)], dma=True)
            for k in range(8):
                fw.emit("pe", lambda e, xq32=xq32, k=k: e.transpose(
                    psT, xq32[:, k * 128:(k + 1) * 128], B["idf"][:, :]),
                    reads=[B["scrA"], B["idf"]], writes=[B["PS1"]])
                fw.emit("dve", lambda e, k=k, rt=rt: e.tensor_copy(
                    FMA[:, k * R + rt * 128:k * R + (rt + 1) * 128], psT),
                    reads=[B["PS1"]], writes=[B["FMA"]])
            for nh in range(2):
                fw.emit("pe", lambda e, xq=xq, nh=nh, rt=rt: e.matmul(
                    PS2[0:1, nh * 512:(nh + 1) * 512], B["onescb"][:, :],
                    xq[:, nh * 512:(nh + 1) * 512],
                    start=(rt % 2 == 0), stop=(rt % 2 == 1)),
                    reads=[B["scrB16"], B["onescb"]], writes=[B["PS2"]])
            if rt % 2 == 1:
                bnum = rt // 2
                pev = B["scrA"][0:1, 1024:2048]
                fw.emit("dve", lambda e, pev=pev: e.tensor_copy(pev, PS2[0:1, 0:1024]),
                        reads=[B["PS2"]], writes=[B["scrA"]])
                dstp = PAFFO[0:1, bnum * 1024:(bnum + 1) * 1024]
                fw.emit("sp", lambda e, dstp=dstp, pev=pev: e.dma_start(
                    out=dstp, in_=pev),
                    reads=[B["scrA"]], writes=[("PAFFO", bnum)], dma=True)
        if B_STOP == "xnorm":
            fw.drain_outputs()
            fw.finalize(block)
            return nc

        FMX = FMA
        for k in range(8):
            src = ins["GW1"][k * 128:(k + 1) * 128, :]
            fw.emit("pool", lambda e, k=k, src=src: e.dma_start(
                out=B["wres"][:, k * 1024:(k + 1) * 1024], in_=src),
                writes=[B["wres"]], dma=True)
        gb1b = B["biasb"][:, 0:1024]
        fw.emit("sp", lambda e: e.dma_start(
            out=gb1b, in_=ins["GB1R"][0:1, :].broadcast_to([128, 1024])),
            writes=[B["biasb"]], dma=True)
        lnsb = B["biasb"][:, 1024:2048]
        fw.emit("sp", lambda e: e.dma_start(
            out=lnsb, in_=ins["LNSR"][0:1, :].broadcast_to([128, 1024])),
            writes=[B["biasb"]], dma=True)
        lnob = B["FMV"][:, 0:2048].bitcast(F32)
        fw.emit("sp", lambda e: e.dma_start(
            out=lnob, in_=ins["LNOR"][0:1, :].broadcast_to([128, 1024])),
            writes=[B["FMV"]], dma=True)
        FMHN = B["FMV"][:, 4096:20480]
        for rt in range(NT):
            for nh in range(2):
                for k in range(8):
                    fw.emit("pe", lambda e, k=k, nh=nh, rt=rt: e.matmul(
                        PS2[:, nh * 512:(nh + 1) * 512],
                        FMX[:, k * R + rt * 128: k * R + (rt + 1) * 128],
                        B["wres"][:, k * 1024 + nh * 512: k * 1024 + (nh + 1) * 512],
                        start=(k == 0), stop=(k == 7)),
                        reads=[B["FMA"], B["wres"]], writes=[B["PS2"]])
            h1 = B["scrA"][:, 0:1024]
            fw.emit("dve", lambda e, h1=h1: e.tensor_tensor(
                h1, PS2[:, 0:1024], gb1b, ALU.add),
                reads=[B["PS2"], B["biasb"]], writes=[B["scrA"]])
            fw.emit("act", lambda e, h1=h1: e.activation(h1, h1, AF.Silu),
                    reads=[B["scrA"]], writes=[B["scrA"]])
            fw.emit("dve", lambda e, h1=h1: e.tensor_reduce(
                st[:, 106:107], h1, AX.X, ALU.add), reads=[B["scrA"]], writes=[st])
            sq = B["scrA"][:, 1024:1536]
            fw.emit("act", lambda e, sq=sq, h1=h1: e.activation(
                sq[:, 0:512], h1[:, 0:512], AF.Square, accum_out=st[:, 107:108]),
                reads=[B["scrA"]], writes=[B["scrA"], st])
            fw.emit("act", lambda e, sq=sq, h1=h1: e.activation(
                sq[:, 0:512], h1[:, 512:1024], AF.Square, accum_out=st[:, 108:109]),
                reads=[B["scrA"]], writes=[B["scrA"], st])
            fw.emit("dve", lambda e: e.tensor_tensor(
                st[:, 107:108], st[:, 107:108], st[:, 108:109], ALU.add),
                reads=[st], writes=[st])
            fw.emit("dve", lambda e: e.tensor_scalar(
                st[:, 106:107], st[:, 106:107], 1.0 / C, None, ALU.mult),
                reads=[st], writes=[st])
            fw.emit("dve", lambda e: e.tensor_scalar(
                st[:, 107:108], st[:, 107:108], 1.0 / C, None, ALU.mult),
                reads=[st], writes=[st])
            fw.emit("dve", lambda e: e.tensor_tensor(
                st[:, 109:110], st[:, 106:107], st[:, 106:107], ALU.mult),
                reads=[st], writes=[st])
            fw.emit("dve", lambda e: e.tensor_tensor(
                st[:, 109:110], st[:, 107:108], st[:, 109:110], ALU.subtract),
                reads=[st], writes=[st])
            fw.emit("dve", lambda e: e.tensor_scalar(
                st[:, 109:110], st[:, 109:110], 1e-5, None, ALU.add),
                reads=[st], writes=[st])
            fw.emit("dve", lambda e: e.reciprocal(st[:, 110:111], st[:, 109:110]),
                    reads=[st], writes=[st])
            fw.emit("act", lambda e: e.activation(
                st[:, 110:111], st[:, 110:111], AF.Sqrt), reads=[st], writes=[st])
            fw.emit("dve", lambda e: e.tensor_scalar(
                st[:, 111:112], st[:, 106:107], -1.0, None, ALU.mult),
                reads=[st], writes=[st])
            fw.emit("dve", lambda e, h1=h1: e.tensor_scalar(
                h1, h1, st[:, 111:112], st[:, 110:111], ALU.add, ALU.mult),
                reads=[B["scrA"], st], writes=[B["scrA"]])
            fw.emit("dve", lambda e, h1=h1: e.tensor_tensor(h1, h1, lnsb, ALU.mult),
                    reads=[B["scrA"], B["biasb"]], writes=[B["scrA"]])
            fw.emit("dve", lambda e, h1=h1: e.tensor_tensor(h1, h1, lnob, ALU.add),
                    reads=[B["scrA"], B["FMV"]], writes=[B["scrA"]])
            for k in range(8):
                fw.emit("pe", lambda e, h1=h1, k=k: e.transpose(
                    psT, h1[:, k * 128:(k + 1) * 128], B["idf"][:, :]),
                    reads=[B["scrA"], B["idf"]], writes=[B["PS1"]])
                fw.emit("dve", lambda e, k=k, rt=rt: e.tensor_copy(
                    FMHN[:, k * R + rt * 128:k * R + (rt + 1) * 128], psT),
                    reads=[B["PS1"]], writes=[B["FMV"]])
        for k in range(8):
            src = ins["GW2"][k * 128:(k + 1) * 128, :]
            fw.emit("pool", lambda e, k=k, src=src: e.dma_start(
                out=B["wres"][:, k * 512:(k + 1) * 512], in_=src),
                writes=[B["wres"]], dma=True)
        gb2b = B["biasb"][:, 0:512]
        fw.emit("sp", lambda e: e.dma_start(
            out=gb2b, in_=ins["GB2R"][0:1, :].broadcast_to([128, 512])),
            writes=[B["biasb"]], dma=True)
        b3b = B["biasb"][:, 512:520]
        fw.emit("sp", lambda e: e.dma_start(
            out=b3b, in_=ins["GB3R"][0:1, :].broadcast_to([128, 8])),
            writes=[B["biasb"]], dma=True)
        for rt in range(NT):
            for k in range(8):
                fw.emit("pe", lambda e, k=k, rt=rt: e.matmul(
                    PS2[:, 0:512],
                    FMHN[:, k * R + rt * 128: k * R + (rt + 1) * 128],
                    B["wres"][:, k * 512:(k + 1) * 512],
                    start=(k == 0), stop=(k == 7)),
                    reads=[B["FMV"], B["wres"]], writes=[B["PS2"]])
            h2 = B["scrA"][:, 0:512]
            fw.emit("dve", lambda e, h2=h2: e.tensor_tensor(
                h2, PS2[:, 0:512], gb2b, ALU.add),
                reads=[B["PS2"], B["biasb"]], writes=[B["scrA"]])
            fw.emit("act", lambda e, h2=h2: e.activation(h2, h2, AF.Silu),
                    reads=[B["scrA"]], writes=[B["scrA"]])
            for j in range(4):
                fw.emit("pe", lambda e, h2=h2, j=j: e.transpose(
                    psT, h2[:, j * 128:(j + 1) * 128], B["idf"][:, :]),
                    reads=[B["scrA"], B["idf"]], writes=[B["PS1"]])
                fw.emit("dve", lambda e, j=j: e.tensor_copy(
                    B["fmt"][:, j * 128:(j + 1) * 128], psT),
                    reads=[B["PS1"]], writes=[B["fmt"]])
            for j in range(4):
                fw.emit("pe", lambda e, j=j: e.matmul(
                    PS2[:, 1024:1032], B["fmt"][:, j * 128:(j + 1) * 128],
                    B["gw3c"][:, j * 8:(j + 1) * 8],
                    start=(j == 0), stop=(j == 3)),
                    reads=[B["fmt"], B["gw3c"]], writes=[B["PS2"]])
            lg = B["lgb"][:, 0:8]
            fw.emit("dve", lambda e, lg=lg: e.tensor_tensor(
                lg, PS2[:, 1024:1032], b3b, ALU.add),
                reads=[B["PS2"], B["biasb"]], writes=[B["lgb"]])
            dst = LGO[rt * 128:(rt + 1) * 128, :]
            fw.emit("sp", lambda e, dst=dst, lg=lg: e.dma_start(out=dst, in_=lg),
                    reads=[B["lgb"]], writes=[("LGO", rt)], dma=True)
        if B_STOP == "gate":
            fw.drain_outputs()
            fw.finalize(block)
            return nc

        for pi, (kind, H1, H2) in enumerate(EXPERTS):
            for hc in range(H1 // 512):
                for k in range(8):
                    src = ins["EW1_%d" % pi][k * 128:(k + 1) * 128,
                                             hc * 512:(hc + 1) * 512]
                    fw.emit("pool", lambda e, k=k, src=src: e.dma_start(
                        out=B["wstr"][:, k * 512:(k + 1) * 512], in_=src),
                        writes=[B["wstr"]], dma=True)
                b1s = B["biasb"][:, 0:512]
                srcb = ins["EB1_%d" % pi][0:1, hc * 512:(hc + 1) * 512]
                fw.emit("sp", lambda e, srcb=srcb, b1s=b1s: e.dma_start(
                    out=b1s, in_=srcb.broadcast_to([128, 512])),
                    writes=[B["biasb"]], dma=True)
                for rt in range(NT):
                    for k in range(8):
                        fw.emit("pe", lambda e, k=k, rt=rt: e.matmul(
                            PS2[:, 0:512],
                            FMX[:, k * R + rt * 128: k * R + (rt + 1) * 128],
                            B["wstr"][:, k * 512:(k + 1) * 512],
                            start=(k == 0), stop=(k == 7)),
                            reads=[B["FMA"], B["wstr"]], writes=[B["PS2"]])
                    hv = B["scrA"][:, 0:512]
                    fw.emit("dve", lambda e, hv=hv, b1s=b1s: e.tensor_tensor(
                        hv, PS2[:, 0:512], b1s, ALU.add),
                        reads=[B["PS2"], B["biasb"]], writes=[B["scrA"]])
                    ht = B["scrB16"][:, 0:512]
                    if kind == "e0":
                        fw.emit("act", lambda e, ht=ht, hv=hv: e.activation(
                            ht, hv, AF.Gelu_apprx_tanh),
                            reads=[B["scrA"]], writes=[B["scrB16"]])
                    elif kind == "e1":
                        fw.emit("act", lambda e, ht=ht, hv=hv: e.activation(
                            ht, hv, AF.Silu),
                            reads=[B["scrA"]], writes=[B["scrB16"]])
                    else:
                        sg = B["scrA"][:, 512:1024]
                        fw.emit("act", lambda e, sg=sg, hv=hv: e.activation(
                            sg, hv, AF.Sigmoid), reads=[B["scrA"]], writes=[B["scrA"]])
                        fw.emit("dve", lambda e, hv=hv, sg=sg: e.tensor_tensor(
                            hv, hv, sg, ALU.mult), reads=[B["scrA"]], writes=[B["scrA"]])
                        fw.emit("dve", lambda e, ht=ht, hv=hv, sg=sg: e.tensor_tensor(
                            ht, hv, sg, ALU.mult),
                            reads=[B["scrA"]], writes=[B["scrB16"]])
                    dsth = hscr[rt * 128:(rt + 1) * 128, hc * 512:(hc + 1) * 512]
                    fw.emit("sp", lambda e, dsth=dsth, ht=ht: e.dma_start(
                        out=dsth, in_=ht), reads=[B["scrB16"]],
                        writes=[("hscr", rt, hc)], dma=True)
            passes = [("EW2_%d" % pi, "EB2_%d" % pi, H1, H2,
                       "silu" if kind == "e1" else None, 0,
                       2048 if kind == "e1" else 0)]
            if kind == "e1":
                passes.append(("EW3_%d" % pi, "EB3_%d" % pi, 2048, 1024,
                               None, 2048, 0))
            for wname, bname, Hin, Hout, actk, srcoff, dstoff in passes:
                nchunk = Hin // 128
                for kc in range(nchunk):
                    src = ins[wname][kc * 128:(kc + 1) * 128, :]
                    fw.emit("pool", lambda e, kc=kc, src=src, Hout=Hout: e.dma_start(
                        out=B["FMV"][:, kc * Hout:(kc + 1) * Hout], in_=src),
                        writes=[B["FMV"]], dma=True)
                b2s = B["biasb"][:, 0:Hout]
                srcb2 = ins[bname][0:1, :]
                fw.emit("sp", lambda e, b2s=b2s, srcb2=srcb2, Hout=Hout: e.dma_start(
                    out=b2s, in_=srcb2.broadcast_to([128, Hout])),
                    writes=[B["biasb"]], dma=True)
                for rt in range(NT):
                    for kc in range(nchunk):
                        fw.emit("sp", lambda e, kc=kc, rt=rt, srcoff=srcoff:
                                e.dma_start_transpose(
                            out=B["fmt"][:, kc * 128:(kc + 1) * 128],
                            in_=hscr[rt * 128:(rt + 1) * 128,
                                     srcoff + kc * 128: srcoff + (kc + 1) * 128]),
                            reads=[("hscr", rt, (srcoff + kc * 128) // 512)],
                            writes=[B["fmt"]], dma=True)
                    nnh = max(1, Hout // 512)
                    for kc in range(nchunk):
                        for nh in range(nnh):
                            nw = min(512, Hout)
                            fw.emit("pe", lambda e, kc=kc, nh=nh, nw=nw, Hout=Hout,
                                    nchunk=nchunk: e.matmul(
                                PS2[:, nh * 512:nh * 512 + nw],
                                B["fmt"][:, kc * 128:(kc + 1) * 128],
                                B["FMV"][:, kc * Hout + nh * 512:
                                         kc * Hout + nh * 512 + nw],
                                start=(kc == 0), stop=(kc == nchunk - 1)),
                                reads=[B["fmt"], B["FMV"]], writes=[B["PS2"]])
                    ov = B["scrA"][:, 0:Hout]
                    fw.emit("dve", lambda e, ov=ov, b2s=b2s, Hout=Hout: e.tensor_tensor(
                        ov, PS2[:, 0:Hout], b2s, ALU.add),
                        reads=[B["PS2"], B["biasb"]], writes=[B["scrA"]])
                    ot16 = B["scrB16"][:, 0:Hout]
                    if actk == "silu":
                        fw.emit("act", lambda e, ot16=ot16, ov=ov: e.activation(
                            ot16, ov, AF.Silu), reads=[B["scrA"]], writes=[B["scrB16"]])
                        dsth = hscr[rt * 128:(rt + 1) * 128, dstoff:dstoff + Hout]
                        fw.emit("sp", lambda e, dsth=dsth, ot16=ot16: e.dma_start(
                            out=dsth, in_=ot16), reads=[B["scrB16"]],
                            writes=[("hscr", rt, dstoff // 512)], dma=True)
                    else:
                        fw.emit("dve", lambda e, ot16=ot16, ov=ov: e.tensor_copy(
                            ot16, ov), reads=[B["scrA"]], writes=[B["scrB16"]])
                        dst = EXO[pi][rt * 128:(rt + 1) * 128, :]
                        fw.emit("sp", lambda e, dst=dst, ot16=ot16: e.dma_start(
                            out=dst, in_=ot16), reads=[B["scrB16"]],
                            writes=[("EXO", pi, rt)], dma=True)
        fw.drain_outputs()
        fw.finalize(block)
    return nc


def _cols128xN(v, n):
    return np.ascontiguousarray(np.asarray(v, np.float32).reshape(n, 128).T)


def prep_B_inputs(I):
    com = {"IDF": np.eye(128, dtype=np.float32)}
    coef = np.concatenate([I["L3_attn_zero"].reshape(8, 1),
                           I["L3_attn_pos"][:, :7]], axis=1).astype(np.float32)
    ccoef = np.zeros((128, 64), np.float32)
    for h in range(8):
        for m in range(8):
            ccoef[:, h * 8 + m] = coef[h, m]
    gw3 = np.asarray(I["L3_ffn_gate_w3"], np.float32)
    gw3c = np.zeros((128, 32), np.float32)
    for j in range(4):
        gw3c[:, j * 8:(j + 1) * 8] = gw3[j * 128:(j + 1) * 128, :]
    com.update({
        "UW2": np.ascontiguousarray(I["L3_attn_Wu"].T),
        "BUC": _cols128xN(I["L3_attn_bu"], 16),
        "VW3": np.ascontiguousarray(I["L3_attn_Wv"].T),
        "BVC": _cols128xN(I["L3_attn_bv"], 16),
        "WO3T": np.ascontiguousarray(I["L3_attn_Wo"].T),
        "BO3R": I["L3_attn_bo"].reshape(1, C).astype(np.float32),
        "CCOEF": ccoef,
        "GW1": np.ascontiguousarray(I["L3_ffn_gate_w1"]),
        "GB1R": I["L3_ffn_gate_b1"].reshape(1, C).astype(np.float32),
        "LNSR": I["L3_ffn_ln_scale"].reshape(1, C).astype(np.float32),
        "LNOR": I["L3_ffn_ln_offset"].reshape(1, C).astype(np.float32),
        "GW2": np.ascontiguousarray(I["L3_ffn_gate_w2"]),
        "GB2R": I["L3_ffn_gate_b2"].reshape(1, 512).astype(np.float32),
        "GW3C": gw3c.astype(np.float32),
        "GB3R": I["L3_ffn_gate_b3"].reshape(1, 8).astype(np.float32),
    })
    TYPEIDX = [0, 0, 0, 1, 1, 1, 2, 2]
    tix = {}
    for pi, (kind, H1, H2) in enumerate(EXPERTS):
        idx = tix.get(kind, 0)
        tix[kind] = idx + 1
        com["EW1_%d" % pi] = np.ascontiguousarray(I["L3_ffn_%s_w1" % kind][idx])
        com["EB1_%d" % pi] = I["L3_ffn_%s_b1" % kind][idx].reshape(1, H1)
        com["EW2_%d" % pi] = np.ascontiguousarray(I["L3_ffn_%s_w2" % kind][idx])
        com["EB2_%d" % pi] = I["L3_ffn_%s_b2" % kind][idx].reshape(1, H2)
        if kind == "e1":
            com["EW3_%d" % pi] = np.ascontiguousarray(I["L3_ffn_e1_w3"][idx])
            com["EB3_%d" % pi] = I["L3_ffn_e1_b3"][idx].reshape(1, C)
    return com


def reshard_tok(x):
    out = []
    for c in range(NC):
        sl = x[:, c * 256:(c + 1) * 256, :]
        out.append(np.ascontiguousarray(sl.reshape(R, x.shape[2])))
    return out


def run_B(I, x3, imp_full):
    nc = build_B()
    com = prep_B_inputs(I)
    impf = np.ascontiguousarray(imp_full.reshape(128, 128).astype(np.float32))
    xsl = reshard_tok(x3.astype(np.float32))
    in_maps = []
    for c in range(NC):
        impsc = np.zeros((128, 16), np.float32)
        sl = imp_full.reshape(8, 2048)[:, c * 256:(c + 1) * 256].reshape(R)
        for rt in range(16):
            impsc[:, rt] = sl[rt * 128:(rt + 1) * 128]
        in_maps.append(dict(com, XB=xsl[c], IMPF=impf,
                            IMPSC=impsc.astype(np.float32)))
    res = run_bass_kernel_spmd(nc, in_maps, core_ids=list(range(NC))).results
    return res



def build_C():
    nc = bass.Bass("TRN2", target_bir_lowering=False)
    ins = {}
    def inp(name, shape, dtype=F32):
        ins[name] = nc.dram_tensor(name, list(shape), dtype, kind="ExternalInput").ap()
    inp("X3S", (R, C)); inp("S3S", (R, C)); inp("XNS", (R, C), BF16)
    inp("LGS", (R, 8)); inp("NSL", (R, 8)); inp("PAFF8", (8, 8 * C))
    inp("USAGE", (1, 8)); inp("AFFWC", (128, 64)); inp("AFFBR", (1, 8))
    inp("IDF", (128, 128))
    for pi in range(8):
        inp("EXS_%d" % pi, (R, C), BF16)
    XO = nc.dram_tensor("XO", [R, C], F32, kind="ExternalOutput").ap()
    affd = nc.dram_tensor("affd", [8, 8], F32, kind="Internal").ap()
    with ExitStack() as ctx:
        E = ctx.enter_context
        fw = FW(nc, ctx)
        B = {}
        B["paff"] = E(nc.sbuf_tensor("paffC", [8, 8 * C], F32))
        B["xmrow"] = E(nc.sbuf_tensor("xmrow", [1, 8 * C], F32))
        B["affw"] = E(nc.sbuf_tensor("affwC", [128, 64], F32))
        B["xmT"] = E(nc.sbuf_tensor("xmT", [128, 64], F32))
        B["scrA"] = E(nc.sbuf_tensor("scrAC", [128, 2048], F32))
        B["scrB"] = E(nc.sbuf_tensor("scrBC", [128, 1024], F32))
        B["ex16"] = E(nc.sbuf_tensor("ex16", [128, 1024], BF16))
        B["g8"] = E(nc.sbuf_tensor("g8", [128, 64], F32))
        B["stats"] = E(nc.sbuf_tensor("statsC", [128, 128], F32))
        B["ones1"] = E(nc.sbuf_tensor("ones1C", [1, 128], F32))
        B["onesc8"] = E(nc.sbuf_tensor("onesc8", [8, 1], F32))
        B["idf"] = E(nc.sbuf_tensor("idfC", [128, 128], F32))
        B["affsb"] = E(nc.sbuf_tensor("affsb", [8, 8], F32))
        PS1 = E(nc.psum_tensor("PS1", [128, 2048], F32))
        B["PS1"] = PS1
        block = E(nc.Block())
        st = B["stats"]
        fw.emit("sp", lambda e: e.dma_start(out=B["idf"][:, :], in_=ins["IDF"]),
                writes=[B["idf"]], dma=True)
        fw.emit("sp", lambda e: e.dma_start(out=B["paff"][:, :], in_=ins["PAFF8"]),
                writes=[B["paff"]], dma=True)
        fw.emit("sp", lambda e: e.dma_start(out=B["affw"][:, :], in_=ins["AFFWC"]),
                writes=[B["affw"]], dma=True)
        fw.emit("sp", lambda e: e.dma_start(out=st[0:1, 0:8], in_=ins["USAGE"]),
                writes=[st], dma=True)
        fw.emit("sp", lambda e: e.dma_start(out=st[0:1, 8:16], in_=ins["AFFBR"]),
                writes=[st], dma=True)
        fw.emit("dve", lambda e: e.memset(B["ones1"][:, :], 1.0), writes=[B["ones1"]])
        fw.emit("dve", lambda e: e.memset(B["onesc8"][:, :], 1.0), writes=[B["onesc8"]])

        for nb in range(16):
            fw.emit("pe", lambda e, nb=nb: e.matmul(
                PS1[0:1, 1024:1536], B["onesc8"][:, :],
                B["paff"][0:8, nb * 512:(nb + 1) * 512], start=True, stop=True),
                reads=[B["paff"], B["onesc8"]], writes=[B["PS1"]])
            fw.emit("dve", lambda e, nb=nb: e.tensor_copy(
                B["xmrow"][0:1, nb * 512:(nb + 1) * 512], PS1[0:1, 1024:1536]),
                reads=[B["PS1"]], writes=[B["xmrow"]])
        for b in range(8):
            for ec in range(8):
                fw.emit("pe", lambda e, b=b, ec=ec: e.transpose(
                    PS1[:, 1536:1537],
                    B["xmrow"][0:1, b * 1024 + ec * 128: b * 1024 + (ec + 1) * 128],
                    B["idf"][0:1, 0:1]),
                    reads=[B["xmrow"], B["idf"]], writes=[B["PS1"]])
                fw.emit("dve", lambda e, b=b, ec=ec: e.tensor_scalar(
                    B["xmT"][:, ec * 8 + b: ec * 8 + b + 1], PS1[:, 1536:1537],
                    1.0 / 2048.0, None, ALU.mult),
                    reads=[B["PS1"]], writes=[B["xmT"]])
        for ec in range(8):
            fw.emit("pe", lambda e, ec=ec: e.matmul(
                PS1[0:8, 1024:1032], B["affw"][:, ec * 8:(ec + 1) * 8],
                B["xmT"][:, ec * 8:(ec + 1) * 8],
                start=(ec == 0), stop=(ec == 7)),
                reads=[B["affw"], B["xmT"]], writes=[B["PS1"]])
        fw.emit("pe", lambda e: e.transpose(PS1[0:8, 1536:1537], st[0:1, 8:16],
                                            B["idf"][0:1, 0:1]),
                reads=[st, B["idf"]], writes=[B["PS1"]])
        fw.emit("dve", lambda e: e.tensor_copy(st[0:8, 16:17], PS1[0:8, 1536:1537]),
                reads=[B["PS1"]], writes=[st])
        fw.emit("dve", lambda e: e.tensor_scalar(
            B["affsb"][:, :], PS1[0:8, 1024:1032], st[0:8, 16:17], None, ALU.add),
            reads=[B["PS1"], st], writes=[B["affsb"]])
        fw.emit("sp", lambda e: e.dma_start(
            out=affd.rearrange("b j -> j b"), in_=B["affsb"][:, :],
            allow_slow_non_contiguous=True),
            reads=[B["affsb"]], writes=[("affd",)], dma=True)

        q8 = st[0:1, 24:32]
        fw.emit("dve", lambda e: e.tensor_scalar(q8, st[0:1, 0:8], 1e-8, None,
                                                 ALU.add), reads=[st], writes=[st])
        fge = st[0:1, 32:40]
        fw.emit("dve", lambda e: e.tensor_scalar(fge, q8, 0.0, None, ALU.is_ge),
                reads=[st], writes=[st])
        F01 = st[0:1, 40:41]
        fw.emit("dve", lambda e: e.tensor_reduce(F01, fge, AX.X, ALU.min),
                reads=[st], writes=[st])
        la = st[0:1, 48:56]
        fw.emit("act", lambda e: e.activation(la, q8, AF.Abs), reads=[st], writes=[st])
        fw.emit("act", lambda e: e.activation(la, la, AF.Ln), reads=[st], writes=[st])
        fw.emit("dve", lambda e: e.tensor_tensor(la, la, st[0:1, 0:8], ALU.mult),
                reads=[st], writes=[st])
        ENT = st[0:1, 41:42]
        fw.emit("dve", lambda e: e.tensor_reduce(ENT, la, AX.X, ALU.add),
                reads=[st], writes=[st])
        NS = st[0:1, 42:43]
        fw.emit("dve", lambda e: e.tensor_scalar(NS, ENT, 2.0, 0.01, ALU.add,
                                                 ALU.mult), reads=[st], writes=[st])
        fw.emit("dve", lambda e: e.tensor_scalar(NS, NS, 1e-4, None, ALU.max),
                reads=[st], writes=[st])
        fw.emit("dve", lambda e: e.tensor_tensor(NS, NS, F01, ALU.mult),
                reads=[st], writes=[st])
        nsc = st[:, 64:65]
        f01c = st[:, 65:66]
        for src11, dcol in ((NS, nsc), (F01, f01c)):
            fw.emit("pe", lambda e, src11=src11: e.matmul(
                PS1[:, 1536:1537], B["ones1"][:, :], src11, start=True, stop=True),
                reads=[st, B["ones1"]], writes=[B["PS1"]])
            fw.emit("dve", lambda e, dcol=dcol: e.tensor_copy(dcol, PS1[:, 1536:1537]),
                    reads=[B["PS1"]], writes=[st])

        BIGN = 1e9
        for rt in range(NT):
            b = rt // 2
            lg = B["g8"][:, 0:8]
            src = ins["LGS"][rt * 128:(rt + 1) * 128, :]
            fw.emit("sp", lambda e, lg=lg, src=src: e.dma_start(out=lg, in_=src),
                    writes=[B["g8"]], dma=True)
            nz = B["g8"][:, 8:16]
            src = ins["NSL"][rt * 128:(rt + 1) * 128, :]
            fw.emit("sp", lambda e, nz=nz, src=src: e.dma_start(out=nz, in_=src),
                    writes=[B["g8"]], dma=True)
            afb = B["g8"][:, 16:24]
            fw.emit("sp", lambda e, afb=afb, b=b: e.dma_start(
                out=afb, in_=affd[b:b + 1, :].broadcast_to([128, 8])),
                reads=[("affd",)], writes=[B["g8"]], dma=True)
            lgn = B["g8"][:, 24:32]
            fw.emit("dve", lambda e, lgn=lgn, nz=nz: e.tensor_scalar(
                lgn, nz, nsc, None, ALU.mult), reads=[B["g8"], st], writes=[B["g8"]])
            fw.emit("dve", lambda e, lgn=lgn, lg=lg: e.tensor_tensor(
                lgn, lgn, lg, ALU.add), reads=[B["g8"]], writes=[B["g8"]])
            fw.emit("dve", lambda e, lgn=lgn, afb=afb: e.scalar_tensor_tensor(
                lgn, afb, 0.3, lgn, ALU.mult, ALU.add),
                reads=[B["g8"]], writes=[B["g8"]])
            fw.emit("dve", lambda e, lgn=lgn: e.tensor_scalar(
                lgn, lgn, 1.0 / 1.5, None, ALU.mult), reads=[B["g8"]], writes=[B["g8"]])
            m1 = st[:, 66:67]
            fw.emit("dve", lambda e, lgn=lgn, m1=m1: e.tensor_reduce(
                m1, lgn, AX.X, ALU.max), reads=[B["g8"]], writes=[st])
            ge1 = B["g8"][:, 32:40]
            fw.emit("dve", lambda e, ge1=ge1, lgn=lgn, m1=m1: e.tensor_scalar(
                ge1, lgn, m1, None, ALU.is_ge), reads=[B["g8"], st], writes=[B["g8"]])
            cnt1 = st[:, 67:68]
            fw.emit("dve", lambda e, ge1=ge1, cnt1=cnt1: e.tensor_reduce(
                cnt1, ge1, AX.X, ALU.add), reads=[B["g8"]], writes=[st])
            t3 = B["g8"][:, 40:48]
            fw.emit("dve", lambda e, t3=t3, ge1=ge1, lgn=lgn: e.scalar_tensor_tensor(
                t3, ge1, -BIGN, lgn, ALU.mult, ALU.add),
                reads=[B["g8"]], writes=[B["g8"]])
            v2c = st[:, 68:69]
            fw.emit("dve", lambda e, t3=t3, v2c=v2c: e.tensor_reduce(
                v2c, t3, AX.X, ALU.max), reads=[B["g8"]], writes=[st])
            c2m = st.bitcast(U8)[:, 69 * 4:69 * 4 + 1]
            fw.emit("dve", lambda e, c2m=c2m, cnt1=cnt1: e.tensor_scalar(
                c2m, cnt1, 2.0, None, ALU.is_ge), reads=[st], writes=[st])
            fw.emit("dve", lambda e, c2m=c2m, v2c=v2c, m1=m1: e.select(
                v2c, c2m, m1, v2c), reads=[st], writes=[st])
            mask01 = B["g8"][:, 48:56]
            fw.emit("dve", lambda e, mask01=mask01, lgn=lgn, v2c=v2c: e.tensor_scalar(
                mask01, lgn, v2c, None, ALU.is_ge),
                reads=[B["g8"], st], writes=[B["g8"]])
            z8 = B["g8"][:, 56:64]
            fw.emit("dve", lambda e, z8=z8, lgn=lgn: e.tensor_scalar(
                z8, lgn, BIGN, None, ALU.add), reads=[B["g8"]], writes=[B["g8"]])
            fw.emit("dve", lambda e, z8=z8, mask01=mask01: e.tensor_tensor(
                z8, z8, mask01, ALU.mult), reads=[B["g8"]], writes=[B["g8"]])
            fw.emit("dve", lambda e, z8=z8: e.tensor_scalar(
                z8, z8, f01c, None, ALU.mult), reads=[B["g8"], st], writes=[B["g8"]])
            fw.emit("dve", lambda e, z8=z8: e.tensor_scalar(
                z8, z8, -BIGN, None, ALU.add), reads=[B["g8"]], writes=[B["g8"]])
            zm = st[:, 70:71]
            fw.emit("dve", lambda e, z8=z8, zm=zm: e.tensor_reduce(
                zm, z8, AX.X, ALU.max), reads=[B["g8"]], writes=[st])
            nzm = st[:, 71:72]
            fw.emit("dve", lambda e, zm=zm, nzm=nzm: e.tensor_scalar(
                nzm, zm, -1.0, None, ALU.mult), reads=[st], writes=[st])
            e8 = B["g8"][:, 0:8]
            fw.emit("act", lambda e, e8=e8, z8=z8, nzm=nzm: e.activation(
                e8, z8, AF.Exp, bias=nzm, accum_out=st[:, 72:73]),
                reads=[B["g8"], st], writes=[B["g8"], st])
            rs = st[:, 73:74]
            fw.emit("dve", lambda e, rs=rs: e.reciprocal(rs, st[:, 72:73]),
                    reads=[st], writes=[st])
            p8 = B["g8"][:, 8:16]
            fw.emit("dve", lambda e, p8=p8, e8=e8, rs=rs: e.tensor_scalar(
                p8, e8, rs, None, ALU.mult), reads=[B["g8"], st], writes=[B["g8"]])
            sel = B["g8"][:, 16:24]
            fw.emit("dve", lambda e, sel=sel, mask01=mask01: e.tensor_scalar(
                sel, mask01, f01c, None, ALU.mult),
                reads=[B["g8"], st], writes=[B["g8"]])
            one_m_f = st[:, 74:75]
            fw.emit("dve", lambda e, one_m_f=one_m_f: e.tensor_scalar(
                one_m_f, f01c, -1.0, 1.0, ALU.mult, ALU.add),
                reads=[st], writes=[st])
            fw.emit("dve", lambda e, sel=sel, one_m_f=one_m_f: e.tensor_scalar(
                sel[:, 0:2], sel[:, 0:2], one_m_f, None, ALU.add),
                reads=[B["g8"], st], writes=[B["g8"]])
            w8 = B["g8"][:, 24:32]
            fw.emit("dve", lambda e, w8=w8, p8=p8, sel=sel: e.tensor_tensor(
                w8, p8, sel, ALU.mult), reads=[B["g8"]], writes=[B["g8"]])
            acc = B["scrA"][:, 0:1024]
            for pi in range(8):
                ext = B["ex16"][:, 0:1024]
                src = ins["EXS_%d" % pi][rt * 128:(rt + 1) * 128, :]
                fw.emit("sp", lambda e, ext=ext, src=src: e.dma_start(
                    out=ext, in_=src), writes=[B["ex16"]], dma=True)
                if pi == 0:
                    fw.emit("dve", lambda e, acc=acc, ext=ext, w8=w8: e.tensor_scalar(
                        acc, ext, w8[:, 0:1], None, ALU.mult),
                        reads=[B["ex16"], B["g8"]], writes=[B["scrA"]])
                else:
                    fw.emit("dve", lambda e, acc=acc, ext=ext, w8=w8, pi=pi:
                            e.scalar_tensor_tensor(
                        acc, ext, w8[:, pi:pi + 1], acc, ALU.mult, ALU.add),
                        reads=[B["ex16"], B["g8"], B["scrA"]], writes=[B["scrA"]])
            xnt = B["ex16"][:, 0:1024]
            src = ins["XNS"][rt * 128:(rt + 1) * 128, :]
            fw.emit("sp", lambda e, xnt=xnt, src=src: e.dma_start(out=xnt, in_=src),
                    writes=[B["ex16"]], dma=True)
            fw.emit("dve", lambda e, acc=acc, xnt=xnt: e.scalar_tensor_tensor(
                acc, xnt, 0.1, acc, ALU.mult, ALU.add),
                reads=[B["ex16"], B["scrA"]], writes=[B["scrA"]])
            g3 = B["scrA"][:, 1024:2048]
            fw.emit("act", lambda e, g3=g3, acc=acc: e.activation(g3, acc, AF.Silu),
                    reads=[B["scrA"]], writes=[B["scrA"]])
            s3t = B["scrB"][:, 0:1024]
            src = ins["S3S"][rt * 128:(rt + 1) * 128, :]
            fw.emit("sp", lambda e, s3t=s3t, src=src: e.dma_start(out=s3t, in_=src),
                    writes=[B["scrB"]], dma=True)
            fw.emit("dve", lambda e, g3=g3, s3t=s3t: e.tensor_tensor(
                g3, g3, s3t, ALU.mult), reads=[B["scrA"], B["scrB"]],
                writes=[B["scrA"]])
            xt = B["scrB"][:, 0:1024]
            src = ins["X3S"][rt * 128:(rt + 1) * 128, :]
            fw.emit("sp", lambda e, xt=xt, src=src: e.dma_start(out=xt, in_=src),
                    writes=[B["scrB"]], dma=True)
            fw.emit("dve", lambda e, xt=xt, g3=g3: e.tensor_tensor(
                xt, xt, g3, ALU.add), reads=[B["scrA"], B["scrB"]], writes=[B["scrB"]])
            dst = XO[rt * 128:(rt + 1) * 128, :]
            fw.emit("sp", lambda e, dst=dst, xt=xt: e.dma_start(out=dst, in_=xt),
                    reads=[B["scrB"]], writes=[("XO", rt)], dma=True)
        fw.drain_outputs()
        fw.finalize(block)
    return nc


def run_C(I, x3, resB, noise):
    nc = build_C()
    affwc = np.zeros((128, 64), np.float32)
    aw = np.asarray(I["L3_ffn_aff_w"], np.float32)
    for ec in range(8):
        affwc[:, ec * 8:(ec + 1) * 8] = aw[ec * 128:(ec + 1) * 128, :]
    paff8 = np.stack([np.asarray(resB[c]["PAFFO"]).reshape(8 * C)
                      for c in range(NC)], 0).astype(np.float32)
    com = {
        "PAFF8": paff8,
        "USAGE": I["L3_ffn_expert_usage"].reshape(1, 8).astype(np.float32),
        "AFFWC": affwc,
        "AFFBR": I["L3_ffn_aff_b"].reshape(1, 8).astype(np.float32),
        "IDF": np.eye(128, dtype=np.float32),
    }
    xsl = reshard_tok(x3.astype(np.float32))
    nsl = reshard_tok(noise.astype(np.float32))
    in_maps = []
    for c in range(NC):
        m = dict(com, X3S=xsl[c], NSL=nsl[c],
                 S3S=np.ascontiguousarray(resB[c]["S3O"]),
                 XNS=np.ascontiguousarray(resB[c]["XNO"]),
                 LGS=np.ascontiguousarray(resB[c]["LGO"]))
        for pi in range(8):
            m["EXS_%d" % pi] = np.ascontiguousarray(resB[c]["EXO_%d" % pi])
        in_maps.append(m)
    res = run_bass_kernel_spmd(nc, in_maps, core_ids=list(range(NC))).results
    out = np.zeros((8, 2048, C), np.float32)
    for c in range(NC):
        out[:, c * 256:(c + 1) * 256, :] = res[c]["XO"].reshape(8, 256, C)
    return out


def kernel(**inputs):
    import jax
    I = inputs
    wq = run_W(I)
    xo, imp = run_A(I, wq, layers=(0, 1, 2), imp_out=True)
    with jax.default_device(jax.devices("cpu")[0]):
        noise = np.asarray(jax.random.normal(jax.random.key(1), (8, 2048, 8)),
                           dtype=np.float32)
    resB = run_B(I, xo, imp)
    x4 = run_C(I, xo, resB, noise)
    return np.ascontiguousarray(x4.astype(np.float32))
